# version 23
# baseline (speedup 1.0000x reference)
import numpy as np
from contextlib import ExitStack

import concourse.bass as bass
import concourse.mybir as mybir
from concourse.bass_utils import run_bass_kernel_spmd

F32 = mybir.dt.float32
BF16 = mybir.dt.bfloat16
OP = mybir.AluOpType
AF = mybir.ActivationFunctionType

B, T, D = 2, 8192, 2048
H = 16
NK = D // 128
QKV = 3 * D
I = 8192
TOK = 2048
TA = 512
NTA = TOK // TA
TC = 512
NTC = TOK // TC
T2 = 512
NT2 = TOK // T2
NSLOT = 24
EPS_LN = 1e-5
EPS_DEN = 1e-6

ENGS = ("t", "v", "s", "g", "y")


def build_program():
    nc = bass.Bass("TRN2", target_bir_lowering=False, debug=False)

    x_fm = nc.dram_tensor("x_fm", [D, TOK], F32, kind="ExternalInput").ap()
    w_qkv = nc.dram_tensor("w_qkv", [D, QKV], F32, kind="ExternalInput").ap()
    w_out = nc.dram_tensor("w_out", [D, D], F32, kind="ExternalInput").ap()
    w_gate = nc.dram_tensor("w_gate", [D, I], F32, kind="ExternalInput").ap()
    w_up = nc.dram_tensor("w_up", [D, I], F32, kind="ExternalInput").ap()
    w_down = nc.dram_tensor("w_down", [I, D], F32, kind="ExternalInput").ap()
    dec16 = nc.dram_tensor("dec16", [16, 1], F32, kind="ExternalInput").ap()
    E16 = nc.dram_tensor("E16", [128, 16 * 128], F32, kind="ExternalInput").ap()
    E4 = nc.dram_tensor("E4", [4, 4 * 128], F32, kind="ExternalInput").ap()
    P16 = nc.dram_tensor("P16", [128, 16 * 16], F32, kind="ExternalInput").ap()
    AR16 = nc.dram_tensor("AR16", [16, TOK], F32, kind="ExternalInput").ap()
    GARR = nc.dram_tensor("GARR", [16, 4], F32, kind="ExternalInput").ap()
    MASK4 = nc.dram_tensor("MASK4", [16, 4], F32, kind="ExternalInput").ap()
    ONES = nc.dram_tensor("ONES", [128, 1], F32, kind="ExternalInput").ap()
    ID128 = nc.dram_tensor("ID128", [128, 128], F32, kind="ExternalInput").ap()
    out_fm = nc.dram_tensor("out_fm", [D, TOK], F32, kind="ExternalOutput").ap()

    wq_bf = nc.dram_tensor("wq_bf", [12 * 16, 128, 512], BF16)
    wo_bf = nc.dram_tensor("wo_bf", [4 * 16, 128, 512], BF16)
    wg_bf = nc.dram_tensor("wg_bf", [16 * 16, 128, 512], BF16)
    wu_bf = nc.dram_tensor("wu_bf", [16 * 16, 128, 512], BF16)
    wd_bf = nc.dram_tensor("wd_bf", [4 * 4 * 16, 128, 512], BF16)
    q_sp = nc.dram_tensor("q_sp", [NTA, 128, H * TA], BF16)
    A_sp = nc.dram_tensor("A_sp", [NTA, 128, H * TA], BF16)
    x2_sp = nc.dram_tensor("x2_sp", [NK, 128, TOK], BF16)
    cc_in = nc.dram_tensor("cc_in", [128, 32], F32)
    cc_out = nc.dram_tensor("cc_out", [4, 128, 32], F32)

    es = ExitStack()

    class Sem:
        __slots__ = ("h", "n", "name")
        def __init__(self, name):
            self.h = es.enter_context(nc.semaphore(name))
            self.n = 0
            self.name = name
    s_wl = [Sem(f"wl{i}") for i in range(4)]
    s_ld = Sem("ld")
    s_ldp = [Sem("ldpa"), Sem("ldpb")]
    s_x = Sem("x")
    s_qa = Sem("qa")
    s_l2 = [Sem(f"l2{i}") for i in range(4)]
    s_fo = [Sem(f"fo{i}") for i in range(4)]
    s_gd = Sem("gd")
    s_p0 = [Sem("p0a"), Sem("p0b")]
    s_sp = [Sem("spa"), Sem("spb")]
    s_qsp = Sem("qsp")
    s_ac = Sem("ac")
    s_x2s = [Sem("x2sa"), Sem("x2sb")]
    s_go = [Sem("goa"), Sem("gob")]
    s_wm = Sem("wm")
    s_g = Sem("g")
    s_t = Sem("t")
    s_v = Sem("v")
    s_s = Sem("s")
    s_cc = Sem("cc")
    s_bar = Sem("bar")

    prog = {e: [] for e in ENGS}

    class _FakeInstr:
        def then_inc(self, *a, **k):
            return self

    class _Rec:
        __slots__ = ("calls",)
        def __init__(self):
            self.calls = []
        def __getattr__(self, name):
            def m(*a, **kw):
                self.calls.append((name, a, kw))
                return _FakeInstr()
            return m

    _SKIP_OPS = {"dma_start", "dma_start_transpose", "wait_ge", "sem_inc",
                 "engine_nop", "nop", "drain", "collective_compute", "print"}
    _WRITE_KW = {"out", "ap", "accum_out", "out_max"}

    def _span(ap):
        try:
            off = ap.offset
            total = 0
            for st, ct in ap.ap:
                total += abs(int(st)) * (int(ct) - 1)
            return (ap.tensor.name, int(off), int(off) + total + 1)
        except Exception:
            return None

    def _foot(callrec):
        import concourse.bass as _b
        name, a, kw = callrec
        rd, wr = [], []
        if name in _SKIP_OPS:
            return rd, wr, False
        argmap = dict(kw)
        for i, v in enumerate(a):
            argmap[f"_pos{i}"] = v
        for k2, v in argmap.items():
            if hasattr(v, "tensor") and hasattr(v, "offset"):
                s = _span(v)
                if s is None:
                    continue
                if k2 in _WRITE_KW or (k2 == "_pos0" and name in ("memset", "memzero")):
                    wr.append(s)
                else:
                    rd.append(s)
        return rd, wr, True

    _win = {e: [] for e in ENGS}
    fence_count = {e: 0 for e in ENGS}

    def _conflict(rd, wr, win):
        for (prd, pwr) in win:
            for w in wr:
                for p in pwr + prd:
                    if w[0] == p[0] and w[1] < p[2] and p[1] < w[2]:
                        return True
            for r in rd:
                for p in pwr:
                    if r[0] == p[0] and r[1] < p[2] and p[1] < r[2]:
                        return True
        return False

    def ins(e, fn, sem=None, k=1):
        if e in ("v", "s", "g"):
            rec = _Rec()
            try:
                fn(rec)
            except Exception:
                rec.calls = []
            if rec.calls:
                rd, wr, tracked = _foot(rec.calls[0])
                if tracked:
                    if _conflict(rd, wr, _win[e]):
                        prog[e].append(lambda eng: eng.drain())
                        fence_count[e] += 1
                        _win[e] = []
                    _win[e].append((rd, wr))
        if sem is None:
            prog[e].append(fn)
        else:
            prog[e].append(lambda eng, f=fn, s=sem.h, kk=k: f(eng).then_inc(s, kk))
            sem.n += k

    def wait(e, sem, val=None):
        v = sem.n if val is None else val
        if v > 0:
            prog[e].append(lambda eng, s=sem.h, vv=v: eng.wait_ge(s, vv))

    def barrier():
        for sem in (s_ld, s_x, s_qa, *s_l2, *s_fo, s_gd, *s_p0, *s_sp, s_qsp,
                    s_ac, *s_x2s, *s_go, s_wm, *s_wl, *s_ldp, s_t, s_v, s_s, s_g):
            wait("g", sem)
        prog["g"].append(lambda eng, s=s_bar.h: eng.sem_inc(s, 1))
        s_bar.n += 1
        for e in ENGS:
            wait(e, s_bar)

    def sb(name, shape, dtype, stack=es):
        return stack.enter_context(nc.sbuf_tensor(name, shape, dtype))

    ps = [es.enter_context(nc.psum_tensor(f"ps{i}", [128, 512], F32)) for i in range(5)]
    ps_st = es.enter_context(nc.psum_tensor("ps_st", [1, 1024], F32))
    ps_den = es.enter_context(nc.psum_tensor("ps_den", [16, 512], F32))

    bank_rel = {}
    def use_bank(b):
        r = bank_rel.get(id(b))
        if r is not None:
            wait("t", r[0], r[1])
    def release_bank(b, sem):
        bank_rel[id(b)] = (sem, sem.n)

    E16f = sb("E16f", [128, 16 * 128], F32)
    E16b = sb("E16b", [128, 16 * 128], BF16)
    E4f = sb("E4f", [4, 4 * 128], F32)
    P16b = sb("P16b", [128, 16 * 16], BF16)
    onesb = sb("onesb", [128, 1], BF16)
    id128f = sb("id128f", [128, 128], F32)
    garrT = sb("garrT", [16, 4], F32)
    maskT = sb("maskT", [16, 4], F32)
    decs = sb("decs", [16, 1], F32)
    scol = sb("scol", [128, 1], F32)
    lnac = sb("lnac", [16, 1], F32)
    c16 = sb("c16", [16, TOK], F32)
    c16b = sb("c16b", [128, TOK], BF16)
    decay_all = sb("decay_all", [128, 16], F32)
    stKV = sb("stKV", [128, 16], F32)
    stK = sb("stK", [128, 16], F32)
    den_loc = sb("den_loc", [16, TOK], F32)
    rankstb = sb("rankstb", [128, 16 * 128], BF16)
    kspad = sb("kspad", [128, 16 * 16], BF16)
    bcLN2 = sb("bcLN2", [128, TOK], F32)
    bc_stat = sb("bc_stat", [128, 512], F32)
    is_b = sb("is_b", [128, 512], F32)
    mis_b = sb("mis_b", [128, 512], F32)
    rows = sb("rows", [1, 512 * 4], F32)
    G = sb("Gg", [128, 128], F32)
    jnk = sb("jnk", [128, 128], F32)
    skv_in = sb("skv_in", [128, 16], F32)
    sk_in = sb("sk_in", [128, 16], F32)
    sk_tmp = sb("sk_tmp", [128, 16], F32)
    wm16 = sb("wm16", [16, 4], F32)
    wm4 = sb("wm4", [128, 16], F32)
    wring = [sb(f"wrs{i}", [128, 512], BF16) for i in range(NSLOT)]

    ring = {"issued": 0, "marks": [], "slots": wring}

    def ring_load(block_ap):
        slots = ring["slots"]
        nslot = len(slots)
        i = ring["issued"]
        if i >= nslot:
            need = i - nslot
            mk = None
            for (thr, sem, val) in ring["marks"]:
                if thr >= need:
                    mk = (sem, val)
                    break
            assert mk is not None, "weight ring under-released"
            wait("y", mk[0], mk[1])
        slot = slots[i % nslot]
        gsem = s_wl[(i // 16) % 4]
        ins("y", lambda e, o=slot, a=block_ap: e.dma_start(out=o[:], in_=a), gsem, 16)
        ring["issued"] += 1
        return slot, (gsem, gsem.n), i

    def ring_release_through(idx, sem):
        ring["marks"].append((idx, sem, sem.n))
        if len(ring["marks"]) > 80:
            ring["marks"] = [m for m in ring["marks"] if m[0] >= ring["issued"] - len(ring["slots"]) - 1]

    init_es = ExitStack()
    AR = sb("AR", [16, TOK], F32, init_es)
    P16f = sb("P16f", [128, 256], F32, init_es)
    onesf = sb("onesf", [128, 1], F32, init_es)
    for dst, src in [(E16f, E16), (E4f, E4), (garrT, GARR), (maskT, MASK4),
                     (decs, dec16), (AR, AR16), (id128f, ID128)]:
        ins("y", lambda e, o=dst, a=src: e.dma_start(out=o[:], in_=a), s_ld, 16)
    ins("y", lambda e: e.dma_start(out=P16f[:], in_=P16), s_ld, 16)
    ins("y", lambda e: e.dma_start(out=onesf[:], in_=ONES), s_ld, 16)
    wait("v", s_ld)
    ins("v", lambda e: e.tensor_copy(out=E16b[:], in_=E16f[:]))
    ins("v", lambda e: e.tensor_copy(out=P16b[:], in_=P16f[:]))
    ins("v", lambda e: e.tensor_copy(out=onesb[:], in_=onesf[:]))
    ins("v", lambda e: e.memset(bc_stat[:], 0.0))
    ins("v", lambda e: e.memset(rankstb[:], 0.0))
    ins("v", lambda e: e.memset(bcLN2[:], 0.0))
    ins("v", lambda e: e.memset(stKV[:], 0.0))
    ins("v", lambda e: e.memset(stK[:], 0.0), s_v)
    ins("v", lambda e: e.memset(scol[:], 0.0), s_v)
    wait("s", s_v)
    ins("s", lambda e: e.activation(out=scol[0:16, :], in_=decs[:], func=AF.Sigmoid), s_s)
    wait("v", s_s)
    ins("v", lambda e: e.tensor_single_scalar(out=scol[0:16, :], in_=scol[0:16, :], scalar=1e-38, op=OP.max), s_v)
    wait("s", s_v)
    ins("s", lambda e: e.activation(out=lnac[:], in_=scol[0:16, :], func=AF.Ln), s_s)
    wait("v", s_s)
    ins("v", lambda e: e.tensor_single_scalar(out=lnac[:], in_=lnac[:], scalar=-1e30, op=OP.max), s_v)
    wait("s", s_v)
    ins("s", lambda e: e.activation(out=c16[:], in_=AR[:], func=AF.Exp, scale=lnac[:]), s_s)
    wait("v", s_s)
    ins("v", lambda e: e.memset(c16b[:], 0.0))
    ins("v", lambda e: e.tensor_copy(out=c16b[0:16, :], in_=c16[:]), s_v)
    wait("t", s_v)
    for h in range(16):
        ins("t", lambda e, hh=h: e.matmul(
            out=ps[0][0:128, hh:hh + 1], lhsT=E16f[:, hh * 128:(hh + 1) * 128],
            rhs=scol[:, 0:1], start=True, stop=True))
    ins("t", lambda e: e.matmul(out=ps[1][:, 0:128], lhsT=E16f[:, 0:128],
                                rhs=E16f[:, 0:128], start=True, stop=True), s_t)
    release_bank(ps[1], s_t)
    wait("v", s_t)
    ins("v", lambda e: e.tensor_copy(out=jnk[:], in_=E16f[:, 0:128]))
    ins("v", lambda e: e.tensor_copy(out=decay_all[:], in_=ps[0][0:128, 0:16]), s_v)
    release_bank(ps[0], s_v)
    barrier()
    init_es.close()

    p0 = {"f32": [sb(f"p0f{i}", [128, 512], F32) for i in range(2)],
          "bf": [sb(f"p0b{i}", [128, 512], BF16) for i in range(2)],
          "i": 0, "tog": 0, "fin": {}, "guard": {0: 0, 1: 0},
          "cguard": {0: None, 1: None}}

    def p0_block(src_ap, dblk):
        i = p0["i"] % 2
        p0["i"] += 1
        f32b, bfb = p0["f32"][i], p0["bf"][i]
        if p0["cguard"][i] is not None:
            wait("s", p0["cguard"][i][0], p0["cguard"][i][1])
        ins("s", lambda e, o=f32b, a=src_ap: e.dma_start(out=o[:], in_=a), s_ldp[i], 16)
        ce = "v" if (p0["tog"] % 2 == 0) else "s"
        p0["tog"] += 1
        wait(ce, s_ldp[i])
        wait(ce, s_p0[i], p0["guard"][i])
        if ce == "v":
            ins("v", lambda e, o=bfb, a=f32b: e.tensor_copy(out=o[:], in_=a[:]), s_v)
            p0["cguard"][i] = (s_v, s_v.n)
            wait("g", s_v)
        else:
            ins("s", lambda e, o=bfb, a=f32b: e.copy(out=o[:], in_=a[:]), s_s)
            p0["cguard"][i] = (s_s, s_s.n)
            wait("g", s_s)
        ins("g", lambda e, d=dblk, b=bfb: e.dma_start(out=d, in_=b[:]), s_p0[i], 16)
        p0["guard"][i] = s_p0[i].n

    def p0_wlike(name, W, scratch, mg_lo, mg_hi):
        for mg in range(mg_lo, mg_hi):
            for k in range(NK):
                src2 = W[k * 128:(k + 1) * 128, mg * 512:(mg + 1) * 512]
                p0_block(src2, scratch.ap()[mg * 16 + k])

    def p0_wd(s_lo, s_hi):
        for s4 in range(s_lo, s_hi):
            for mg in range(4):
                for ci in range(16):
                    c = 16 * s4 + ci
                    j, m = c // 4, c % 4
                    src2 = w_down[512 * j + m: 512 * j + m + 509: 4, mg * 512:(mg + 1) * 512]
                    idx = s4 * 64 + mg * 16 + ci
                    p0_block(src2, wd_bf.ap()[idx])

    def p0_fin(name):
        p0["fin"][name] = (s_p0[0].n, s_p0[1].n)

    def ln_rows_and_bcast(store_isrow=None, store_misrow=None):
        tv = s_t.n
        wait("v", s_t, tv)
        inv_n = 1.0 / D
        mu = rows[0:1, 0:512]
        e2 = rows[0:1, 512:1024]
        var = rows[0:1, 1024:1536]
        mis = rows[0:1, 1536:2048]
        ins("v", lambda e: e.tensor_single_scalar(out=mu, in_=ps_st[0:1, 0:512], scalar=inv_n, op=OP.mult))
        ins("v", lambda e: e.tensor_single_scalar(out=e2, in_=ps_st[0:1, 512:1024], scalar=inv_n, op=OP.mult))
        ins("v", lambda e: e.tensor_tensor(out=var, in0=mu, in1=mu, op=OP.mult))
        ins("v", lambda e: e.tensor_tensor(out=var, in0=e2, in1=var, op=OP.subtract))
        ins("v", lambda e: e.tensor_single_scalar(out=var, in_=var, scalar=EPS_LN, op=OP.add), s_v)
        release_bank(ps_st, s_v)
        wait("s", s_v)
        ins("s", lambda e: e.activation(out=e2, in_=var, func=AF.Ln), s_s)
        isdst = bc_stat[0:1, 0:512] if store_isrow is None else store_isrow
        wait("s", s_s)
        wait("s", s_t)
        ins("s", lambda e, d=isdst: e.activation(out=d, in_=e2, func=AF.Exp, scale=-0.5), s_s)
        wait("v", s_s)
        wait("v", s_gd)
        ins("v", lambda e, d=isdst: e.tensor_tensor(out=mis, in0=mu, in1=d, op=OP.mult), s_v)
        wait("g", s_v)
        misdst = bc_stat[1:2, 0:512] if store_misrow is None else store_misrow
        ins("g", lambda e, d=misdst: e.dma_start(out=d, in_=mis), s_gd, 16)
        if store_isrow is None:
            wait("t", s_gd)
            bcast_pair_from(bc_stat[:, 0:512])

    def bcast_pair_from(bc_slice):
        use_bank(ps[0])
        use_bank(ps[1])
        ins("t", lambda e, a=bc_slice: e.matmul(out=ps[0][:], lhsT=E16f[:, 0:128], rhs=a, start=True, stop=True))
        ins("t", lambda e, a=bc_slice: e.matmul(out=ps[1][:], lhsT=E16f[:, 128:256], rhs=a, start=True, stop=True), s_t)
        wait("v", s_t)
        ins("v", lambda e: e.tensor_copy(out=is_b[:], in_=ps[0][:]))
        ins("v", lambda e: e.tensor_copy(out=mis_b[:], in_=ps[1][:]), s_v)
        release_bank(ps[0], s_v)
        release_bank(ps[1], s_v)

    p0_wlike("wq", w_qkv, wq_bf, 0, 12)
    p0_fin("wq")

    a_es = ExitStack()
    xc = [sb(f"xc{i}", [128, TA], F32, a_es) for i in range(4)]
    xcb = [sb(f"xcb{i}", [128, TA], BF16, a_es) for i in range(2)]
    xcs = [sb(f"xcs{i}", [128, TA], BF16, a_es) for i in range(2)]
    bcLN1 = sb("bcLN1", [128, TOK], F32, a_es)
    ln1x2 = [sb(f"ln1x{i}", [128, NK * TA], BF16, a_es) for i in range(2)]
    qtile = sb("qtile", [128, H * TA], BF16, a_es)
    ktile = sb("ktile", [128, H * TA], BF16, a_es)
    Abuf = [sb(f"Abuf{i}", [128, TA], BF16, a_es) for i in range(2)]
    prodb = sb("prodb", [128, H * TA], BF16, a_es)
    kvt = sb("kvt", [128, TA + 1], F32, a_es)
    kvloc = sb("kvloc", [128, TA + 1], F32, a_es)
    ksloc = sb("ksloc", [128, TA + 1], F32, a_es)
    phs = sb("phs", [128, TA + 1], F32, a_es)
    ksin = phs
    tmpa = sb("tmpa", [128, TA], F32, a_es)

    marks = {}
    ins("v", lambda e: e.memset(bcLN1[:], 0.0), s_v)

    for ta in range(NTA):
        t0 = ta * TA
        for k in range(NK):
            xck = xc[k % 4]
            gk = ("ln1c", ta, k - 4) if k >= 4 else (("ln1c", ta - 1, 12 + k) if ta > 0 else None)
            if gk is not None and gk in marks:
                wait("y", s_v, marks[gk][0])
                wait("y", s_s, marks[gk][1])
            xin = x_fm[k * 128:(k + 1) * 128, t0:t0 + TA]
            ins("y", lambda e, a=xin, o=xck: e.dma_start(out=o[:], in_=a), s_l2[k % 4], 16)
            cb, sq = xcb[k % 2], xcs[k % 2]
            gk2 = ("ln1col", ta, k - 2) if k >= 2 else (("ln1col", ta - 1, 14 + k) if ta > 0 else None)
            wait("v", s_l2[k % 4])
            wait("s", s_l2[k % 4])
            if gk2 is not None and gk2 in marks:
                wait("v", s_t, marks[gk2])
                wait("s", s_t, marks[gk2])
            ins("v", lambda e, o=cb, a=xck: e.tensor_copy(out=o[:], in_=a[:]), s_v)
            ins("s", lambda e, o=sq, a=xck: e.activation(out=o[:], in_=a[:], func=AF.Square), s_s)
            vv, sv = s_v.n, s_s.n
            marks[("ln1c", ta, k)] = (s_v.n, s_s.n)
            if k == 0:
                use_bank(ps_st)
            wait("t", s_v, vv)
            wait("t", s_s, sv)
            ins("t", lambda e, a=cb, st=(k == 0), sp=(k == NK - 1): e.matmul(
                out=ps_st[0:1, 0:512], lhsT=onesb[:], rhs=a[:], start=st, stop=sp))
            ins("t", lambda e, a=sq, st=(k == 0), sp=(k == NK - 1): e.matmul(
                out=ps_st[0:1, 512:1024], lhsT=onesb[:], rhs=a[:], start=st, stop=sp), s_t)
            marks[("ln1col", ta, k)] = s_t.n
        ln_rows_and_bcast(store_isrow=bcLN1[0:1, t0:t0 + TA], store_misrow=bcLN1[1:2, t0:t0 + TA])

    def a_apply(ta):
        t0 = ta * TA
        steps = []
        def bcast_step():
            wait("t", s_gd)
            bcast_pair_from(bcLN1[:, t0:t0 + TA])
            marks[("a_bc", ta)] = s_v.n
        steps.append(bcast_step)
        for k in range(NK):
            def chunk(k=k):
                xck = xc[k % 4]
                if k >= 4:
                    gk = ("ln1ap", ta, k - 4)
                elif ta > 0:
                    gk = ("ln1ap", ta - 1, 12 + k)
                else:
                    gk = ("ln1c", NTA - 1, 12 + k)
                if gk in marks:
                    mv = marks[gk]
                    if isinstance(mv, tuple):
                        wait("g", s_v, mv[0])
                        wait("g", s_s, mv[1])
                    else:
                        wait("g", s_v, mv)
                xin = x_fm[k * 128:(k + 1) * 128, t0:t0 + TA]
                ins("g", lambda e, a=xin, o=xck: e.dma_start(out=o[:], in_=a), s_l2[k % 4], 16)
                wait("v", s_l2[k % 4])
                if k == 0:
                    wait("v", s_v, marks[("a_bc", ta)])
                ins("v", lambda e, a=xck: e.tensor_tensor(out=tmpa[:], in0=a[:], in1=is_b[:], op=OP.mult))
                ins("v", lambda e, k=k, o=ln1x2[ta % 2]: e.tensor_tensor(
                    out=o[:, k * TA:(k + 1) * TA], in0=tmpa[:], in1=mis_b[:], op=OP.subtract), s_v)
                marks[("ln1ap", ta, k)] = s_v.n
            steps.append(chunk)
        def fin():
            marks[("ln1x_rdy", ta)] = s_v.n
        steps.append(fin)
        return steps

    for st_ in a_apply(0):
        st_()
    marks[("ln1x_rdy", 0)] = s_v.n

    wait("y", s_p0[0], p0["fin"]["wq"][0])
    wait("y", s_p0[1], p0["fin"]["wq"][1])
    for ta in range(NTA):
        t0 = ta * TA
        ln1x = ln1x2[ta % 2]
        nxt = a_apply(ta + 1) if ta + 1 < NTA else []
        nxt_i = 0
        if ta > 0:
            wait("v", s_qsp, marks[("qsp", ta - 1)])
        wait("t", s_v, marks[("ln1x_rdy", ta)])
        for mq in range(48):
            mg = mq // 4
            if mq % 4 == 0:
                grp_blocks = [ring_load(wq_bf.ap()[mg * 16 + k]) for k in range(NK)]
                wait("t", grp_blocks[-1][1][0], grp_blocks[-1][1][1])
            if mq % 3 == 0 and nxt_i < len(nxt):
                nxt[nxt_i]()
                nxt_i += 1
            pb = ps[mq % 5]
            use_bank(pb)
            for k in range(NK):
                ins("t", lambda e, w=grp_blocks[k][0], k=k, mq=mq, p=pb, lx=ln1x: e.matmul(
                    out=p[:], lhsT=w[:, (mq % 4) * 128:(mq % 4 + 1) * 128],
                    rhs=lx[:, k * TA:(k + 1) * TA],
                    start=(k == 0), stop=(k == NK - 1)),
                    s_t if k == NK - 1 else None)
            if mq % 4 == 3:
                ring_release_through(grp_blocks[-1][2], s_t)
            mm_done = s_t.n
            h = mq % 16
            if mq < 16:
                wait("v", s_t, mm_done)
                ins("v", lambda e, p=pb: e.tensor_single_scalar(out=phs[:, 0:TA], in_=p[:], scalar=0.0, op=OP.min), s_v)
                wait("s", s_v)
                ins("s", lambda e: e.activation(out=kvloc[:, 0:TA], in_=phs[:, 0:TA], func=AF.Exp), s_s)
                ins("v", lambda e, p=pb: e.tensor_single_scalar(out=tmpa[:], in_=p[:], scalar=0.0, op=OP.max), s_v)
                release_bank(pb, s_v)
                wait("v", s_s)
                ins("v", lambda e, h=h: e.tensor_tensor(out=qtile[:, h * TA:(h + 1) * TA], in0=kvloc[:, 0:TA], in1=tmpa[:], op=OP.add), s_v)
            elif mq < 32:
                wait("v", s_t, mm_done)
                ins("v", lambda e, p=pb: e.tensor_single_scalar(out=phs[:, 0:TA], in_=p[:], scalar=0.0, op=OP.min), s_v)
                wait("s", s_v)
                ins("s", lambda e: e.activation(out=kvloc[:, 0:TA], in_=phs[:, 0:TA], func=AF.Exp), s_s)
                ins("v", lambda e, p=pb: e.tensor_single_scalar(out=tmpa[:], in_=p[:], scalar=0.0, op=OP.max), s_v)
                release_bank(pb, s_v)
                wait("v", s_s)
                ins("v", lambda e, h=h: e.tensor_tensor(out=ktile[:, h * TA:(h + 1) * TA], in0=kvloc[:, 0:TA], in1=tmpa[:], op=OP.add))
                ins("v", lambda e, h=h: e.tensor_copy(out=ksin[:, 0:1], in_=stK[:, h:h + 1]))
                ins("v", lambda e, h=h: e.tensor_copy(out=ksin[:, 1:TA + 1], in_=ktile[:, h * TA:(h + 1) * TA]))
                ins("v", lambda e, h=h: e.tensor_tensor_scan(
                    out=ksloc[:], data0=decay_all[:, h:h + 1].broadcast_to([128, TA + 1]),
                    data1=ksin[:], initial=0.0, op0=OP.mult, op1=OP.add))
                ins("v", lambda e, h=h: e.tensor_tensor(out=prodb[:, h * TA:(h + 1) * TA], in0=qtile[:, h * TA:(h + 1) * TA], in1=ksloc[:, 1:TA + 1], op=OP.mult))
                ins("v", lambda e, h=h: e.tensor_copy(out=stK[:, h:h + 1], in_=ksloc[:, TA:TA + 1]), s_v)
            else:
                wait("v", s_t, mm_done)
                ins("v", lambda e, h=h: e.tensor_copy(out=kvt[:, 0:1], in_=stKV[:, h:h + 1]))
                ins("v", lambda e, p=pb, h=h: e.tensor_tensor(out=kvt[:, 1:TA + 1], in0=ktile[:, h * TA:(h + 1) * TA], in1=p[:], op=OP.mult), s_v)
                release_bank(pb, s_v)
                ins("v", lambda e, h=h: e.tensor_tensor_scan(
                    out=kvloc[:], data0=decay_all[:, h:h + 1].broadcast_to([128, TA + 1]),
                    data1=kvt[:], initial=0.0, op0=OP.mult, op1=OP.add))
                ab = Abuf[h % 2]
                gk = ("Asp", ta, h - 2) if h >= 2 else (("Asp", ta - 1, 14 + h) if ta > 0 else None)
                if gk is not None and gk in marks:
                    wait("v", s_sp[h % 2], marks[gk])
                ins("v", lambda e, h=h, ab=ab: e.tensor_tensor(out=ab[:], in0=qtile[:, h * TA:(h + 1) * TA], in1=kvloc[:, 1:TA + 1], op=OP.mult))
                ins("v", lambda e, h=h: e.tensor_copy(out=stKV[:, h:h + 1], in_=kvloc[:, TA:TA + 1]), s_v)
                wait("g", s_v)
                ins("g", lambda e, h=h, ab=ab, ta=ta: e.dma_start(out=A_sp.ap()[ta][:, h * TA:(h + 1) * TA], in_=ab[:]), s_sp[h % 2], 16)
                marks[("Asp", ta, h)] = s_sp[h % 2].n
        while nxt_i < len(nxt):
            nxt[nxt_i]()
            nxt_i += 1
        if ta + 1 < NTA:
            marks[("ln1x_rdy", ta + 1)] = s_v.n
        use_bank(ps_den)
        wait("t", s_v)
        for h in range(16):
            ins("t", lambda e, h=h: e.matmul(
                out=ps_den[:], lhsT=P16b[:, h * 16:(h + 1) * 16], rhs=prodb[:, h * TA:(h + 1) * TA],
                start=(h == 0), stop=(h == 15)), s_t if h == 15 else None)
        marks[("denmm", ta)] = s_t.n
        wait("v", s_t)
        ins("v", lambda e, t0=t0: e.tensor_copy(out=den_loc[:, t0:t0 + TA], in_=ps_den[:]), s_v)
        release_bank(ps_den, s_v)
        wait("g", s_v)
        ins("g", lambda e, ta=ta: e.dma_start(out=q_sp.ap()[ta], in_=qtile[:]), s_qsp, 16)
        marks[("qsp", ta)] = s_qsp.n
        if ta == 0:
            p0_wlike("wo", w_out, wo_bf, 0, 4)
            p0_fin("wo")
        elif ta == 1:
            p0_wlike("wg1", w_gate, wg_bf, 0, 16)
        elif ta == 2:
            p0_wlike("wu1", w_up, wu_bf, 0, 16)
        elif ta == 3:
            p0_wd(0, 4)
            p0_fin("ffn")

    barrier()
    ins("g", lambda e: e.dma_start(out=cc_in.ap()[:, 0:16], in_=stKV[:]), s_gd, 16)
    ins("g", lambda e: e.dma_start(out=cc_in.ap()[:, 16:32], in_=stK[:]), s_gd, 16)
    wait("g", s_gd)
    ins("g", lambda e: e.collective_compute(
        "AllGather", OP.bypass, replica_groups=[[0, 1, 2, 3], [4, 5, 6, 7]],
        ins=[cc_in.ap().opt()], outs=[cc_out.ap().opt()]), s_cc, 1)
    wait("g", s_cc)
    for r in range(4):
        ins("g", lambda e, r=r: e.dma_start(out=G[:, r * 32:(r + 1) * 32], in_=cc_out.ap()[r]), s_gd, 16)
    gv = s_gd.n
    wait("s", s_gd, gv)
    ins("s", lambda e: e.activation(out=wm16[:], in_=garrT[:], func=AF.Exp, scale=lnac[:]), s_s)
    wait("v", s_s)
    ins("v", lambda e: e.tensor_tensor(out=wm16[:], in0=wm16[:], in1=maskT[:], op=OP.mult), s_v)
    ins("v", lambda e: e.memset(wm4[:], 0.0), s_v)
    wait("g", s_v)
    for r in range(4):
        ins("g", lambda e, r=r: e.dma_start(out=wm4[r:r + 1, 0:16], in_=wm16[0:16, r:r + 1]), s_wm, 16)
    wait("t", s_wm)
    use_bank(ps[2])
    use_bank(ps[3])
    for r in range(4):
        ins("t", lambda e, r=r: e.matmul(out=ps[2][0:128, r * 16:(r + 1) * 16],
                                         lhsT=E16f[:, r * 128:(r + 1) * 128],
                                         rhs=wm4[:], start=True, stop=True))
    ins("t", lambda e: e.matmul(out=ps[3][:, 0:128], lhsT=E16f[:, 0:128],
                                rhs=E16f[:, 0:128], start=True, stop=True), s_t)
    release_bank(ps[3], s_t)
    wait("v", s_t)
    ins("v", lambda e: e.tensor_copy(out=jnk[:], in_=E16f[:, 0:128]))
    for r in range(4):
        if r == 0:
            ins("v", lambda e, r=r: e.tensor_tensor(out=skv_in[:], in0=G[:, r * 32:r * 32 + 16], in1=ps[2][0:128, r * 16:r * 16 + 16], op=OP.mult))
            ins("v", lambda e, r=r: e.tensor_tensor(out=sk_in[:], in0=G[:, r * 32 + 16:r * 32 + 32], in1=ps[2][0:128, r * 16:r * 16 + 16], op=OP.mult))
        else:
            ins("v", lambda e, r=r: e.tensor_tensor(out=sk_tmp[:], in0=G[:, r * 32:r * 32 + 16], in1=ps[2][0:128, r * 16:r * 16 + 16], op=OP.mult))
            ins("v", lambda e: e.tensor_tensor(out=skv_in[:], in0=skv_in[:], in1=sk_tmp[:], op=OP.add))
            ins("v", lambda e, r=r: e.tensor_tensor(out=sk_tmp[:], in0=G[:, r * 32 + 16:r * 32 + 32], in1=ps[2][0:128, r * 16:r * 16 + 16], op=OP.mult))
            ins("v", lambda e: e.tensor_tensor(out=sk_in[:], in0=sk_in[:], in1=sk_tmp[:], op=OP.add))
    ins("v", lambda e: e.engine_nop(), s_v)
    release_bank(ps[2], s_v)
    ins("v", lambda e: e.memset(kspad[:], 0.0))
    ins("v", lambda e: e.tensor_copy(out=kspad[:, 0:16], in_=sk_in[:]))
    for h in range(15, 0, -1):
        ins("v", lambda e, h=h: e.tensor_copy(out=kspad[:, 16 * h + h:16 * h + h + 1], in_=kspad[:, h:h + 1]))
    ins("v", lambda e: e.memset(kspad[:, 1:16], 0.0), s_v)
    wait("g", s_v)
    for h in range(16):
        ins("g", lambda e, h=h: e.dma_start(out=G[h:h + 1, 0:128], in_=skv_in[:, h:h + 1],
                                            allow_slow_non_contiguous=True), s_wm, 16)
    wait("v", s_wm)
    for h in range(16):
        ins("v", lambda e, h=h: e.tensor_tensor(out=rankstb[0:16, h * 128:(h + 1) * 128],
                                                in0=G[0:16, 0:128], in1=E16f[0:16, h * 128:(h + 1) * 128], op=OP.mult),
            s_v if h == 15 else None)
    barrier()
    a_es.close()

    c1_es = ExitStack()
    qtc = sb("qtc", [128, H * TC], BF16, c1_es)
    Atc = sb("Atc", [128, H * TC], BF16, c1_es)
    attnt2 = [sb(f"attnt{i}", [128, H * TC], BF16, c1_es) for i in range(2)]
    yt = sb("yt", [128, NK * TC], F32, c1_es)
    ybc = [sb(f"ybc{i}", [128, TC], BF16, c1_es) for i in range(2)]
    ysq = [sb(f"ysq{i}", [128, TC], BF16, c1_es) for i in range(2)]
    x2bc = [sb(f"x2bc{i}", [128, TC], BF16, c1_es) for i in range(2)]
    x2sq = [sb(f"x2sq{i}", [128, TC], BF16, c1_es) for i in range(2)]
    dent = sb("dent", [16, TC], F32, c1_es)
    r16b = sb("r16b", [128, TC], BF16, c1_es)
    tmpc = sb("tmpc", [128, TC], F32, c1_es)

    wait("y", s_p0[0], p0["fin"]["wo"][0])
    wait("y", s_p0[1], p0["fin"]["wo"][1])
    ins("v", lambda e: e.memset(r16b[:], 0.0), s_v)

    def c1_head(tc):
        t0 = tc * TC
        attnt = attnt2[tc % 2]
        if tc > 0:
            wait("y", s_v, marks[("c1_free", tc - 1)])
        ins("y", lambda e, tc=tc: e.dma_start(out=qtc[:], in_=q_sp.ap()[tc]), s_qa, 16)
        ins("y", lambda e, tc=tc: e.dma_start(out=Atc[:], in_=A_sp.ap()[tc]), s_qa, 16)
        qa_ready = s_qa.n
        use_bank(ps_den)
        wait("t", s_qa, qa_ready)
        for h in range(16):
            ins("t", lambda e, h=h: e.matmul(out=ps_den[:], lhsT=kspad[:, h * 16:(h + 1) * 16],
                                             rhs=qtc[:, h * TC:(h + 1) * TC], start=(h == 0), stop=(h == 15)),
                s_t if h == 15 else None)
        wait("v", s_t)
        ins("v", lambda e, t0=t0: e.tensor_tensor(out=dent[:], in0=c16[:, t0:t0 + TC], in1=ps_den[:], op=OP.mult), s_v)
        release_bank(ps_den, s_v)
        ins("v", lambda e, t0=t0: e.tensor_tensor(out=dent[:], in0=dent[:], in1=den_loc[:, t0:t0 + TC], op=OP.add))
        ins("v", lambda e: e.tensor_single_scalar(out=dent[:], in_=dent[:], scalar=EPS_DEN, op=OP.max))
        ins("v", lambda e: e.reciprocal(out=dent[:], in_=dent[:]))
        ins("v", lambda e: e.tensor_copy(out=r16b[0:16, :], in_=dent[:]), s_v)
        rdy = s_v.n
        if tc > 1:
            wait("v", s_t, marks[("wout_mm", tc - 2)])
        wait("t", s_v, rdy)
        for h in range(16):
            pc = ps[2] if h % 2 == 0 else ps[3]
            pr = ps[4]
            use_bank(pc)
            ins("t", lambda e, h=h, t0=t0, p=pc: e.matmul(
                out=p[:], lhsT=rankstb[:, h * 128:(h + 1) * 128], rhs=c16b[:, t0:t0 + TC], start=True, stop=True), s_t)
            use_bank(pr)
            ins("t", lambda e, h=h, p=pr: e.matmul(
                out=p[:], lhsT=E16b[:, h * 128:(h + 1) * 128], rhs=r16b[:], start=True, stop=True), s_t)
            mmv = s_t.n
            wait("v", s_t, mmv)
            ins("v", lambda e, h=h, p=pc: e.tensor_tensor(out=tmpc[:], in0=qtc[:, h * TC:(h + 1) * TC], in1=p[:], op=OP.mult), s_v)
            release_bank(pc, s_v)
            ins("v", lambda e, h=h: e.tensor_tensor(out=tmpc[:], in0=tmpc[:], in1=Atc[:, h * TC:(h + 1) * TC], op=OP.add))
            ins("v", lambda e, h=h, a=attnt, p=pr: e.tensor_tensor(out=a[:, h * TC:(h + 1) * TC], in0=tmpc[:], in1=p[:], op=OP.mult), s_v)
            release_bank(pr, s_v)
        marks[("c1_free", tc)] = s_v.n
        marks[("attn_rdy", tc)] = s_v.n

    def c1_wout(tc):
        attnt = attnt2[tc % 2]
        wait("t", s_v, marks[("attn_rdy", tc)])
        for mo in range(16):
            mg = mo // 4
            if mo % 4 == 0:
                grp_blocks = [ring_load(wo_bf.ap()[mg * 16 + k]) for k in range(NK)]
                wait("t", grp_blocks[-1][1][0], grp_blocks[-1][1][1])
            pb = ps[mo % 2]
            use_bank(pb)
            for k in range(NK):
                ins("t", lambda e, w=grp_blocks[k][0], k=k, mo=mo, p=pb: e.matmul(
                    out=p[:], lhsT=w[:, (mo % 4) * 128:(mo % 4 + 1) * 128],
                    rhs=attnt[:, k * TC:(k + 1) * TC], start=(k == 0), stop=(k == NK - 1)),
                    s_t if k == NK - 1 else None)
            if mo % 4 == 3:
                ring_release_through(grp_blocks[-1][2], s_t)
            mmv = s_t.n
            yb, sq = ybc[mo % 2], ysq[mo % 2]
            gk = ("c1col", tc, mo - 2) if mo >= 2 else (("c1col", tc - 1, 14 + mo) if tc > 0 else None)
            if gk is not None and gk in marks:
                wait("s", s_t, marks[gk])
            wait("s", s_t, mmv)
            ins("s", lambda e, mo=mo, p=pb: e.copy(out=yt[:, mo * TC:(mo + 1) * TC], in_=p[:]))
            ins("s", lambda e, p=pb, o=yb: e.copy(out=o[:], in_=p[:]))
            ins("s", lambda e, p=pb, o=sq: e.activation(out=o[:], in_=p[:], func=AF.Square), s_s)
            release_bank(pb, s_s)
            sv = s_s.n
            if mo == 0:
                use_bank(ps_st)
            wait("t", s_s, sv)
            ins("t", lambda e, a=yb, st=(mo == 0), sp=(mo == 15): e.matmul(
                out=ps_st[0:1, 0:512], lhsT=onesb[:], rhs=a[:], start=st, stop=sp))
            ins("t", lambda e, a=sq, st=(mo == 0), sp=(mo == 15): e.matmul(
                out=ps_st[0:1, 512:1024], lhsT=onesb[:], rhs=a[:], start=st, stop=sp), s_t)
            marks[("c1col", tc, mo)] = s_t.n
        marks[("wout_mm", tc)] = s_t.n

    def c1_tail(tc):
        t0 = tc * TC
        ln_rows_and_bcast()
        wait("v", s_v)
        for k in range(NK):
            ins("v", lambda e, k=k: e.tensor_tensor(out=tmpc[:], in0=yt[:, k * TC:(k + 1) * TC], in1=is_b[:], op=OP.mult))
            ins("v", lambda e, k=k: e.tensor_tensor(out=yt[:, k * TC:(k + 1) * TC], in0=tmpc[:], in1=mis_b[:], op=OP.subtract),
                s_v if k == NK - 1 else None)
        wait("g", s_v)
        for k in range(NK):
            xin = x_fm[k * 128:(k + 1) * 128, t0:t0 + TC]
            ins("g", lambda e, k=k, a=xin: e.dma_start(out=yt[:, k * TC:(k + 1) * TC], in_=a, accum_op=OP.add), s_ac, 16)
        acc_done = s_ac.n
        wait("s", s_ac, acc_done)
        wait("v", s_ac, acc_done)
        for k in range(NK):
            xb2, sq2 = x2bc[k % 2], x2sq[k % 2]
            gk = ("ln2col", tc, k - 2) if k >= 2 else (("ln2col", tc - 1, 14 + k) if tc > 0 else None)
            if gk is not None and gk in marks:
                wait("s", s_t, marks[gk])
                wait("v", s_t, marks[gk])
            gk2 = ("x2spill", tc, k - 2) if k >= 2 else (("x2spill", tc - 1, 14 + k) if tc > 0 else None)
            if gk2 is not None and gk2 in marks:
                wait("s", s_x2s[k % 2], marks[gk2])
            ins("s", lambda e, k=k, o=xb2: e.copy(out=o[:], in_=yt[:, k * TC:(k + 1) * TC]), s_s)
            wait("v", s_s)
            ins("v", lambda e, o=sq2, a=xb2: e.tensor_tensor(out=o[:], in0=a[:], in1=a[:], op=OP.mult), s_v)
            sv, vv = s_s.n, s_v.n
            wait("g", s_s, sv)
            ins("g", lambda e, k=k, a=xb2, tc=tc: e.dma_start(out=x2_sp.ap()[k][:, tc * TC:(tc + 1) * TC], in_=a[:]), s_x2s[k % 2], 16)
            marks[("x2spill", tc, k)] = s_x2s[k % 2].n
            if k == 0:
                use_bank(ps_st)
            wait("t", s_s, sv)
            wait("t", s_v, vv)
            ins("t", lambda e, a=xb2, st=(k == 0), sp=(k == NK - 1): e.matmul(
                out=ps_st[0:1, 0:512], lhsT=onesb[:], rhs=a[:], start=st, stop=sp))
            ins("t", lambda e, a=sq2, st=(k == 0), sp=(k == NK - 1): e.matmul(
                out=ps_st[0:1, 512:1024], lhsT=onesb[:], rhs=a[:], start=st, stop=sp), s_t)
            marks[("ln2col", tc, k)] = s_t.n
        ln_rows_and_bcast(store_isrow=bcLN2[0:1, t0:t0 + TC], store_misrow=bcLN2[1:2, t0:t0 + TC])

    c1_head(0)
    for tc in range(NTC):
        c1_wout(tc)
        if tc + 1 < NTC:
            c1_head(tc + 1)
        c1_tail(tc)

    barrier()
    c1_es.close()

    c2_es = ExitStack()
    wring2 = [sb(f"wr2_{i}", [128, 512], BF16, c2_es) for i in range(14)]
    ring["slots"] = wring + wring2
    ring["issued"] = 0
    ring["marks"] = [(10 ** 9, s_t, s_t.n)]
    ring["marks"] = []
    ln2x = sb("ln2x", [128, NK * T2], BF16, c2_es)
    xv2 = [sb(f"xv2_{i}", [128, T2], BF16, c2_es) for i in range(4)]
    hslab = sb("hslab", [128, 16 * T2], BF16, c2_es)
    facc = sb("facc", [128, NK * T2], F32, c2_es)
    hm = [sb(f"hm{i}", [128, T2], F32, c2_es) for i in range(4)]
    gs = [sb(f"gs{i}", [128, T2], F32, c2_es) for i in range(4)]
    m2t = sb("m2t", [128, T2], F32, c2_es)
    m2u = sb("m2u", [128, T2], F32, c2_es)
    yb3 = [sb(f"yb3_{i}", [128, T2], BF16, c2_es) for i in range(2)]
    ys3 = [sb(f"ys3_{i}", [128, T2], BF16, c2_es) for i in range(2)]
    obuf = [sb(f"obuf{i}", [128, T2], F32, c2_es) for i in range(2)]
    tmp2 = sb("tmp2", [128, T2], F32, c2_es)

    wait("y", s_p0[0], p0["fin"]["ffn"][0])
    wait("y", s_p0[1], p0["fin"]["ffn"][1])
    for t2 in range(NT2):
        t0 = t2 * T2
        bcast_pair_from(bcLN2[:, t0:t0 + T2])
        bcv = s_v.n
        if t2 > 0:
            wait("v", s_t, marks[("ffn_mm", t2 - 1)])
            wait("v", s_s, marks[("ln3cast", t2 - 1)])
        for k in range(NK):
            xb2 = xv2[k % 4]
            gk = ("ln2x", t2, k - 4) if k >= 4 else (("fout_v", t2 - 1, 12 + k) if t2 > 0 else None)
            if gk is not None and gk in marks:
                wait("y", s_v, marks[gk])
            ins("y", lambda e, k=k, t0=t0, o=xb2: e.dma_start(out=o[:], in_=x2_sp.ap()[k][:, t0:t0 + T2]), s_l2[k % 4], 16)
            wait("v", s_l2[k % 4])
            if k == 0:
                wait("v", s_v, bcv)
            ins("v", lambda e, k=k, a=xb2: e.tensor_tensor(out=tmp2[:], in0=a[:], in1=is_b[:], op=OP.mult))
            ins("v", lambda e, k=k: e.tensor_tensor(out=ln2x[:, k * T2:(k + 1) * T2], in0=tmp2[:], in1=mis_b[:], op=OP.subtract), s_v)
            marks[("ln2x", t2, k)] = s_v.n
        ln2_ready = s_v.n
        for s4 in range(4):
            if s4 > 0:
                wait("v", s_t, marks[("down_mm", t2, s4 - 1)])
            for jj in range(4):
                j = 4 * s4 + jj
                if s4 + jj > 0:
                    wait("s", s_v, marks.get(("spars", t2, s4, jj - 1),
                                             marks.get(("spars", t2, s4 - 1, 3), 0)))
                gblocks = [ring_load(wg_bf.ap()[j * 16 + k]) for k in range(NK)]
                wait("t", gblocks[-1][1][0], gblocks[-1][1][1])
                for m in range(4):
                    pg = ps[[2, 3, 4, 0][m]]
                    use_bank(pg)
                    for k in range(NK):
                        if s4 == 0 and jj == 0 and m == 0:
                            wait("t", s_v, marks[("ln2x", t2, k)])
                        ins("t", lambda e, w=gblocks[k][0], k=k, m=m, p=pg: e.matmul(
                            out=p[:], lhsT=w[:, m::4], rhs=ln2x[:, k * T2:(k + 1) * T2],
                            start=(k == 0), stop=(k == NK - 1)), s_t if k == NK - 1 else None)
                    gv2 = s_t.n
                    wait("s", s_t, gv2)
                    ins("s", lambda e, m=m, p=pg: e.activation(out=gs[m][:], in_=p[:], func=AF.Silu), s_s)
                    release_bank(pg, s_s)
                ring_release_through(gblocks[-1][2], s_t)
                ublocks = [ring_load(wu_bf.ap()[j * 16 + k]) for k in range(NK)]
                wait("t", ublocks[-1][1][0], ublocks[-1][1][1])
                for m in range(4):
                    pu = ps[[2, 3, 4, 0][m]]
                    use_bank(pu)
                    for k in range(NK):
                        ins("t", lambda e, w=ublocks[k][0], k=k, m=m, p=pu: e.matmul(
                            out=p[:], lhsT=w[:, m::4], rhs=ln2x[:, k * T2:(k + 1) * T2],
                            start=(k == 0), stop=(k == NK - 1)), s_t if k == NK - 1 else None)
                    uv = s_t.n
                    wait("v", s_t, uv)
                    wait("v", s_s)
                    ins("v", lambda e, m=m, p=pu: e.tensor_tensor(out=hm[m][:], in0=gs[m][:], in1=p[:], op=OP.mult), s_v)
                    release_bank(pu, s_v)
                ring_release_through(ublocks[-1][2], s_t)
                hm_done = s_v.n
                wait("s", s_v, hm_done)
                for m in range(4):
                    ins("s", lambda e, m=m: e.activation(out=gs[m][:], in_=hm[m][:], func=AF.Square),
                        s_s if m == 3 else None)
                wait("v", s_s)
                ins("v", lambda e: e.tensor_tensor(out=m2t[:], in0=gs[0][:], in1=gs[1][:], op=OP.max))
                ins("v", lambda e: e.tensor_tensor(out=m2u[:], in0=gs[2][:], in1=gs[3][:], op=OP.max))
                ins("v", lambda e: e.tensor_tensor(out=m2t[:], in0=m2t[:], in1=m2u[:], op=OP.min))
                ins("v", lambda e: e.tensor_tensor(out=m2u[:], in0=gs[0][:], in1=gs[1][:], op=OP.min))
                ins("v", lambda e: e.tensor_tensor(out=tmp2[:], in0=gs[2][:], in1=gs[3][:], op=OP.min))
                ins("v", lambda e: e.tensor_tensor(out=m2u[:], in0=m2u[:], in1=tmp2[:], op=OP.max))
                ins("v", lambda e: e.tensor_tensor(out=m2t[:], in0=m2t[:], in1=m2u[:], op=OP.max))
                for m in range(4):
                    ins("v", lambda e, m=m: e.tensor_tensor(out=tmp2[:], in0=gs[m][:], in1=m2t[:], op=OP.is_ge))
                    ins("v", lambda e, m=m, jj=jj: e.tensor_tensor(
                        out=hslab[:, (4 * jj + m) * T2:(4 * jj + m + 1) * T2], in0=tmp2[:], in1=hm[m][:], op=OP.mult),
                        s_v if m == 3 else None)
                marks[("spars", t2, s4, jj)] = s_v.n
            hs_ready = s_v.n
            wait("t", s_v, hs_ready)
            for mg in range(4):
                dblocks = [ring_load(wd_bf.ap()[s4 * 64 + mg * 16 + ci]) for ci in range(16)]
                wait("t", dblocks[-1][1][0], dblocks[-1][1][1])
                for mo in range(4 * mg, 4 * mg + 4):
                    pd = ps[[2, 3, 4, 0][mo % 4]]
                    use_bank(pd)
                    for ci in range(16):
                        ins("t", lambda e, w=dblocks[ci][0], ci=ci, mo=mo, mg=mg, p=pd: e.matmul(
                            out=p[:], lhsT=w[:, (mo - 4 * mg) * 128:(mo - 4 * mg + 1) * 128],
                            rhs=hslab[:, ci * T2:(ci + 1) * T2], start=(ci == 0), stop=(ci == 15)),
                            s_t if ci == 15 else None)
                    dv = s_t.n
                    wait("v", s_t, dv)
                    if s4 == 0:
                        ins("v", lambda e, mo=mo, p=pd: e.tensor_copy(out=facc[:, mo * T2:(mo + 1) * T2], in_=p[:]), s_v)
                    else:
                        ins("v", lambda e, mo=mo, p=pd: e.tensor_tensor(
                            out=facc[:, mo * T2:(mo + 1) * T2], in0=facc[:, mo * T2:(mo + 1) * T2], in1=p[:], op=OP.add), s_v)
                    release_bank(pd, s_v)
                ring_release_through(dblocks[-1][2], s_t)
            marks[("down_mm", t2, s4)] = s_t.n
        ffn_ready = s_v.n
        marks[("ffn_mm", t2)] = s_t.n
        wait("s", s_v, ffn_ready)
        for k in range(NK):
            yb, sqq = yb3[k % 2], ys3[k % 2]
            gk = ("ln3col", t2, k - 2) if k >= 2 else (("ln3col", t2 - 1, 14 + k) if t2 > 0 else None)
            if gk is not None and gk in marks:
                wait("s", s_t, marks[gk])
            ins("s", lambda e, k=k, o=yb: e.copy(out=o[:], in_=facc[:, k * T2:(k + 1) * T2]))
            ins("s", lambda e, k=k, o=sqq: e.activation(out=o[:], in_=facc[:, k * T2:(k + 1) * T2], func=AF.Square), s_s)
            sv = s_s.n
            if k == 0:
                use_bank(ps_st)
            wait("t", s_s, sv)
            ins("t", lambda e, a=yb, st=(k == 0), sp=(k == NK - 1): e.matmul(
                out=ps_st[0:1, 0:512], lhsT=onesb[:], rhs=a[:], start=st, stop=sp))
            ins("t", lambda e, a=sqq, st=(k == 0), sp=(k == NK - 1): e.matmul(
                out=ps_st[0:1, 512:1024], lhsT=onesb[:], rhs=a[:], start=st, stop=sp), s_t)
            marks[("ln3col", t2, k)] = s_t.n
        marks[("ln3cast", t2)] = s_s.n
        ln_rows_and_bcast()
        wait("v", s_v)
        for k in range(NK):
            xb2 = xv2[k % 4]
            ob = obuf[k % 2]
            gko = ("fout_g", t2, k - 2) if k >= 2 else (("fout_g", t2 - 1, 14 + k) if t2 > 0 else None)
            if gko is not None and gko in marks:
                wait("v", s_go[k % 2], marks[gko])
            gkx = ("fout_v", t2, k - 4) if k >= 4 else (("ln2x", t2, 12 + k))
            if gkx in marks:
                wait("y", s_v, marks[gkx])
            ins("y", lambda e, k=k, t0=t0, o=xb2: e.dma_start(out=o[:], in_=x2_sp.ap()[k][:, t0:t0 + T2]), s_fo[k % 4], 16)
            ins("v", lambda e, k=k: e.tensor_tensor(out=tmp2[:], in0=facc[:, k * T2:(k + 1) * T2], in1=is_b[:], op=OP.mult))
            ins("v", lambda e, k=k: e.tensor_tensor(out=tmp2[:], in0=tmp2[:], in1=mis_b[:], op=OP.subtract))
            wait("v", s_fo[k % 4])
            ins("v", lambda e, k=k, a=xb2, o=ob: e.tensor_tensor(out=o[:], in0=tmp2[:], in1=a[:], op=OP.add), s_v)
            marks[("fout_v", t2, k)] = s_v.n
            wait("g", s_v)
            ins("g", lambda e, k=k, t0=t0, o=ob: e.dma_start(
                out=out_fm[k * 128:(k + 1) * 128, t0:t0 + T2], in_=o[:]), s_go[k % 2], 16)
            marks[("fout_g", t2, k)] = s_go[k % 2].n

    wait("g", s_go[0])
    wait("g", s_go[1])
    wait("g", s_gd)
    c2_es.close()

    with nc.Block() as block:
        @block.sync
        def _(e):
            for f in prog["y"]:
                f(e)
        @block.tensor
        def _(e):
            for f in prog["t"]:
                f(e)
        @block.vector
        def _(e):
            for f in prog["v"]:
                f(e)
        @block.scalar
        def _(e):
            for f in prog["s"]:
                f(e)
        @block.gpsimd
        def _(e):
            for f in prog["g"]:
                f(e)
    es.close()
    import sys as _s
    print(f"[kernel] auto-fences: v={fence_count['v']} s={fence_count['s']}", file=_s.stderr)
    return nc


_NC_CACHE = {}


def kernel(L0_attn_x, L0_attn_w_qkv, L0_attn_w_out, L0_attn_decay_param,
           L0_ffn_w_gate, L0_ffn_w_up, L0_ffn_w_down):
    x = np.asarray(L0_attn_x, np.float32)
    w_qkv_a = np.ascontiguousarray(np.asarray(L0_attn_w_qkv, np.float32))
    w_out_a = np.ascontiguousarray(np.asarray(L0_attn_w_out, np.float32))
    w_gate_a = np.ascontiguousarray(np.asarray(L0_ffn_w_gate, np.float32))
    w_up_a = np.ascontiguousarray(np.asarray(L0_ffn_w_up, np.float32))
    w_down_a = np.ascontiguousarray(np.asarray(L0_ffn_w_down, np.float32))
    dec = np.ascontiguousarray(np.asarray(L0_attn_decay_param, np.float32).reshape(16, 1))

    E16a = np.zeros((128, 16 * 128), np.float32)
    for h in range(16):
        E16a[h, h * 128:(h + 1) * 128] = 1.0
    E4a = np.zeros((4, 4 * 128), np.float32)
    for r in range(4):
        E4a[r, r * 128:(r + 1) * 128] = 1.0
    P16a = np.zeros((128, 16 * 16), np.float32)
    for h in range(16):
        P16a[:, h * 16 + h] = 1.0
    AR16a = np.tile(np.arange(1, TOK + 1, dtype=np.float32), (16, 1))
    ONESa = np.ones((128, 1), np.float32)
    ID128a = np.eye(128, dtype=np.float32)

    if "nc" not in _NC_CACHE:
        _NC_CACHE["nc"] = build_program()
    nc = _NC_CACHE["nc"]

    in_maps = []
    for c in range(8):
        seq, pos = c // 4, c % 4
        x_fm_c = np.ascontiguousarray(x[seq, pos * TOK:(pos + 1) * TOK, :].T)
        garr = np.zeros((16, 4), np.float32)
        mask = np.zeros((16, 4), np.float32)
        for r in range(4):
            if r < pos:
                garr[:, r] = 2048.0 * (pos - 1 - r)
                mask[:, r] = 1.0
        in_maps.append({
            "x_fm": x_fm_c, "w_qkv": w_qkv_a, "w_out": w_out_a, "w_gate": w_gate_a,
            "w_up": w_up_a, "w_down": w_down_a, "dec16": dec, "E16": E16a, "E4": E4a,
            "P16": P16a, "AR16": AR16a, "GARR": garr, "MASK4": mask,
            "ONES": ONESa, "ID128": ID128a,
        })
    res = run_bass_kernel_spmd(nc, in_maps, core_ids=list(range(8)))
    out = np.empty((B, T, D), np.float32)
    for c in range(8):
        seq, pos = c // 4, c % 4
        out[seq, pos * TOK:(pos + 1) * TOK, :] = res.results[c]["out_fm"].T
    return out


# version 24
# speedup vs baseline: 1.0033x; 1.0033x over previous
import numpy as np
from contextlib import ExitStack

import concourse.bass as bass
import concourse.mybir as mybir
from concourse.bass_utils import run_bass_kernel_spmd

F32 = mybir.dt.float32
BF16 = mybir.dt.bfloat16
OP = mybir.AluOpType
AF = mybir.ActivationFunctionType

B, T, D = 2, 8192, 2048
H = 16
NK = D // 128
QKV = 3 * D
I = 8192
TOK = 2048
TA = 512
NTA = TOK // TA
TC = 512
NTC = TOK // TC
T2 = 512
NT2 = TOK // T2
NSLOT = 24
EPS_LN = 1e-5
EPS_DEN = 1e-6

ENGS = ("t", "v", "s", "g", "y")


def build_program():
    nc = bass.Bass("TRN2", target_bir_lowering=False, debug=False)

    x_fm = nc.dram_tensor("x_fm", [D, TOK], F32, kind="ExternalInput").ap()
    w_qkv = nc.dram_tensor("w_qkv", [D, QKV], F32, kind="ExternalInput").ap()
    w_out = nc.dram_tensor("w_out", [D, D], F32, kind="ExternalInput").ap()
    w_gate = nc.dram_tensor("w_gate", [D, I], F32, kind="ExternalInput").ap()
    w_up = nc.dram_tensor("w_up", [D, I], F32, kind="ExternalInput").ap()
    w_down = nc.dram_tensor("w_down", [I, D], F32, kind="ExternalInput").ap()
    dec16 = nc.dram_tensor("dec16", [16, 1], F32, kind="ExternalInput").ap()
    E16 = nc.dram_tensor("E16", [128, 16 * 128], F32, kind="ExternalInput").ap()
    E4 = nc.dram_tensor("E4", [4, 4 * 128], F32, kind="ExternalInput").ap()
    P16 = nc.dram_tensor("P16", [128, 16 * 16], F32, kind="ExternalInput").ap()
    AR16 = nc.dram_tensor("AR16", [16, TOK], F32, kind="ExternalInput").ap()
    GARR = nc.dram_tensor("GARR", [16, 4], F32, kind="ExternalInput").ap()
    MASK4 = nc.dram_tensor("MASK4", [16, 4], F32, kind="ExternalInput").ap()
    ONES = nc.dram_tensor("ONES", [128, 1], F32, kind="ExternalInput").ap()
    ID128 = nc.dram_tensor("ID128", [128, 128], F32, kind="ExternalInput").ap()
    out_fm = nc.dram_tensor("out_fm", [D, TOK], F32, kind="ExternalOutput").ap()

    wq_bf = nc.dram_tensor("wq_bf", [12 * 16, 128, 512], BF16)
    wo_bf = nc.dram_tensor("wo_bf", [4 * 16, 128, 512], BF16)
    wg_bf = nc.dram_tensor("wg_bf", [16 * 16, 128, 512], BF16)
    wu_bf = nc.dram_tensor("wu_bf", [16 * 16, 128, 512], BF16)
    wd_bf = nc.dram_tensor("wd_bf", [4 * 4 * 16, 128, 512], BF16)
    q_sp = nc.dram_tensor("q_sp", [NTA, 128, H * TA], BF16)
    A_sp = nc.dram_tensor("A_sp", [NTA, 128, H * TA], BF16)
    x2_sp = nc.dram_tensor("x2_sp", [NK, 128, TOK], BF16)
    cc_in = nc.dram_tensor("cc_in", [128, 32], F32)
    cc_out = nc.dram_tensor("cc_out", [4, 128, 32], F32)

    es = ExitStack()

    class Sem:
        __slots__ = ("h", "n", "name")
        def __init__(self, name):
            self.h = es.enter_context(nc.semaphore(name))
            self.n = 0
            self.name = name
    s_wl = [Sem(f"wl{i}") for i in range(4)]
    s_ld = Sem("ld")
    s_ldp = [Sem("ldpa"), Sem("ldpb")]
    s_x = Sem("x")
    s_qa = Sem("qa")
    s_l2 = [Sem(f"l2{i}") for i in range(4)]
    s_ax = [Sem(f"ax{i}") for i in range(4)]
    s_fo = [Sem(f"fo{i}") for i in range(4)]
    s_gd = Sem("gd")
    s_p0 = [Sem("p0a"), Sem("p0b")]
    s_sp = [Sem("spa"), Sem("spb")]
    s_qsp = Sem("qsp")
    s_ac = Sem("ac")
    s_x2s = [Sem("x2sa"), Sem("x2sb")]
    s_go = [Sem("goa"), Sem("gob")]
    s_wm = Sem("wm")
    s_g = Sem("g")
    s_t = Sem("t")
    s_v = Sem("v")
    s_s = Sem("s")
    s_cc = Sem("cc")
    s_bar = Sem("bar")

    prog = {e: [] for e in ENGS}

    class _FakeInstr:
        def then_inc(self, *a, **k):
            return self

    class _Rec:
        __slots__ = ("calls",)
        def __init__(self):
            self.calls = []
        def __getattr__(self, name):
            def m(*a, **kw):
                self.calls.append((name, a, kw))
                return _FakeInstr()
            return m

    _SKIP_OPS = {"dma_start", "dma_start_transpose", "wait_ge", "sem_inc",
                 "engine_nop", "nop", "drain", "collective_compute", "print"}
    _WRITE_KW = {"out", "ap", "accum_out", "out_max"}

    def _span(ap):
        try:
            off = ap.offset
            total = 0
            for st, ct in ap.ap:
                total += abs(int(st)) * (int(ct) - 1)
            return (ap.tensor.name, int(off), int(off) + total + 1)
        except Exception:
            return None

    def _foot(callrec):
        import concourse.bass as _b
        name, a, kw = callrec
        rd, wr = [], []
        if name in _SKIP_OPS:
            return rd, wr, False
        argmap = dict(kw)
        for i, v in enumerate(a):
            argmap[f"_pos{i}"] = v
        for k2, v in argmap.items():
            if hasattr(v, "tensor") and hasattr(v, "offset"):
                s = _span(v)
                if s is None:
                    continue
                if k2 in _WRITE_KW or (k2 == "_pos0" and name in ("memset", "memzero")):
                    wr.append(s)
                else:
                    rd.append(s)
        return rd, wr, True

    _win = {e: [] for e in ENGS}
    fence_count = {e: 0 for e in ENGS}

    def _conflict(rd, wr, win):
        for (prd, pwr) in win:
            for w in wr:
                for p in pwr + prd:
                    if w[0] == p[0] and w[1] < p[2] and p[1] < w[2]:
                        return True
            for r in rd:
                for p in pwr:
                    if r[0] == p[0] and r[1] < p[2] and p[1] < r[2]:
                        return True
        return False

    def ins(e, fn, sem=None, k=1):
        if e in ("v", "s", "g"):
            rec = _Rec()
            try:
                fn(rec)
            except Exception:
                rec.calls = []
            if rec.calls:
                rd, wr, tracked = _foot(rec.calls[0])
                if tracked:
                    if _conflict(rd, wr, _win[e]):
                        prog[e].append(lambda eng: eng.drain())
                        fence_count[e] += 1
                        _win[e] = []
                    _win[e].append((rd, wr))
        if sem is None:
            prog[e].append(fn)
        else:
            prog[e].append(lambda eng, f=fn, s=sem.h, kk=k: f(eng).then_inc(s, kk))
            sem.n += k

    def wait(e, sem, val=None):
        v = sem.n if val is None else val
        if v > 0:
            prog[e].append(lambda eng, s=sem.h, vv=v: eng.wait_ge(s, vv))

    def barrier():
        for sem in (s_ld, s_x, s_qa, *s_l2, *s_fo, s_gd, *s_p0, *s_sp, s_qsp,
                    s_ac, *s_x2s, *s_go, s_wm, *s_wl, *s_ldp, *s_ax, s_t, s_v, s_s, s_g):
            wait("g", sem)
        prog["g"].append(lambda eng, s=s_bar.h: eng.sem_inc(s, 1))
        s_bar.n += 1
        for e in ENGS:
            wait(e, s_bar)

    def sb(name, shape, dtype, stack=es):
        return stack.enter_context(nc.sbuf_tensor(name, shape, dtype))

    ps = [es.enter_context(nc.psum_tensor(f"ps{i}", [128, 512], F32)) for i in range(5)]
    ps_st = es.enter_context(nc.psum_tensor("ps_st", [1, 1024], F32))
    ps_den = es.enter_context(nc.psum_tensor("ps_den", [16, 512], F32))

    bank_rel = {}
    def use_bank(b):
        r = bank_rel.get(id(b))
        if r is not None:
            wait("t", r[0], r[1])
    def release_bank(b, sem):
        bank_rel[id(b)] = (sem, sem.n)

    E16f = sb("E16f", [128, 16 * 128], F32)
    E16b = sb("E16b", [128, 16 * 128], BF16)
    E4f = sb("E4f", [4, 4 * 128], F32)
    P16b = sb("P16b", [128, 16 * 16], BF16)
    onesb = sb("onesb", [128, 1], BF16)
    id128f = sb("id128f", [128, 128], F32)
    garrT = sb("garrT", [16, 4], F32)
    maskT = sb("maskT", [16, 4], F32)
    decs = sb("decs", [16, 1], F32)
    scol = sb("scol", [128, 1], F32)
    lnac = sb("lnac", [16, 1], F32)
    c16 = sb("c16", [16, TOK], F32)
    c16b = sb("c16b", [128, TOK], BF16)
    decay_all = sb("decay_all", [128, 16], F32)
    stKV = sb("stKV", [128, 16], F32)
    stK = sb("stK", [128, 16], F32)
    den_loc = sb("den_loc", [16, TOK], F32)
    rankstb = sb("rankstb", [128, 16 * 128], BF16)
    kspad = sb("kspad", [128, 16 * 16], BF16)
    bcLN2 = sb("bcLN2", [128, TOK], F32)
    bc_stat = sb("bc_stat", [128, 512], F32)
    is_b = sb("is_b", [128, 512], F32)
    mis_b = sb("mis_b", [128, 512], F32)
    rows = sb("rows", [1, 512 * 4], F32)
    G = sb("Gg", [128, 128], F32)
    jnk = sb("jnk", [128, 128], F32)
    skv_in = sb("skv_in", [128, 16], F32)
    sk_in = sb("sk_in", [128, 16], F32)
    sk_tmp = sb("sk_tmp", [128, 16], F32)
    wm16 = sb("wm16", [16, 4], F32)
    wm4 = sb("wm4", [128, 16], F32)
    wring = [sb(f"wrs{i}", [128, 512], BF16) for i in range(NSLOT)]

    ring = {"issued": 0, "marks": [], "slots": wring}

    def ring_load(block_ap):
        slots = ring["slots"]
        nslot = len(slots)
        i = ring["issued"]
        if i >= nslot:
            need = i - nslot
            mk = None
            for (thr, sem, val) in ring["marks"]:
                if thr >= need:
                    mk = (sem, val)
                    break
            assert mk is not None, "weight ring under-released"
            wait("y", mk[0], mk[1])
        slot = slots[i % nslot]
        gsem = s_wl[(i // 16) % 4]
        ins("y", lambda e, o=slot, a=block_ap: e.dma_start(out=o[:], in_=a), gsem, 16)
        ring["issued"] += 1
        return slot, (gsem, gsem.n), i

    def ring_release_through(idx, sem):
        ring["marks"].append((idx, sem, sem.n))
        if len(ring["marks"]) > 80:
            ring["marks"] = [m for m in ring["marks"] if m[0] >= ring["issued"] - len(ring["slots"]) - 1]

    init_es = ExitStack()
    AR = sb("AR", [16, TOK], F32, init_es)
    P16f = sb("P16f", [128, 256], F32, init_es)
    onesf = sb("onesf", [128, 1], F32, init_es)
    for dst, src in [(E16f, E16), (E4f, E4), (garrT, GARR), (maskT, MASK4),
                     (decs, dec16), (AR, AR16), (id128f, ID128)]:
        ins("y", lambda e, o=dst, a=src: e.dma_start(out=o[:], in_=a), s_ld, 16)
    ins("y", lambda e: e.dma_start(out=P16f[:], in_=P16), s_ld, 16)
    ins("y", lambda e: e.dma_start(out=onesf[:], in_=ONES), s_ld, 16)
    wait("v", s_ld)
    ins("v", lambda e: e.tensor_copy(out=E16b[:], in_=E16f[:]))
    ins("v", lambda e: e.tensor_copy(out=P16b[:], in_=P16f[:]))
    ins("v", lambda e: e.tensor_copy(out=onesb[:], in_=onesf[:]))
    ins("v", lambda e: e.memset(bc_stat[:], 0.0))
    ins("v", lambda e: e.memset(rankstb[:], 0.0))
    ins("v", lambda e: e.memset(bcLN2[:], 0.0))
    ins("v", lambda e: e.memset(stKV[:], 0.0))
    ins("v", lambda e: e.memset(stK[:], 0.0), s_v)
    ins("v", lambda e: e.memset(scol[:], 0.0), s_v)
    wait("s", s_v)
    ins("s", lambda e: e.activation(out=scol[0:16, :], in_=decs[:], func=AF.Sigmoid), s_s)
    wait("v", s_s)
    ins("v", lambda e: e.tensor_single_scalar(out=scol[0:16, :], in_=scol[0:16, :], scalar=1e-38, op=OP.max), s_v)
    wait("s", s_v)
    ins("s", lambda e: e.activation(out=lnac[:], in_=scol[0:16, :], func=AF.Ln), s_s)
    wait("v", s_s)
    ins("v", lambda e: e.tensor_single_scalar(out=lnac[:], in_=lnac[:], scalar=-1e30, op=OP.max), s_v)
    wait("s", s_v)
    ins("s", lambda e: e.activation(out=c16[:], in_=AR[:], func=AF.Exp, scale=lnac[:]), s_s)
    wait("v", s_s)
    ins("v", lambda e: e.memset(c16b[:], 0.0))
    ins("v", lambda e: e.tensor_copy(out=c16b[0:16, :], in_=c16[:]), s_v)
    wait("t", s_v)
    for h in range(16):
        ins("t", lambda e, hh=h: e.matmul(
            out=ps[0][0:128, hh:hh + 1], lhsT=E16f[:, hh * 128:(hh + 1) * 128],
            rhs=scol[:, 0:1], start=True, stop=True))
    ins("t", lambda e: e.matmul(out=ps[1][:, 0:128], lhsT=E16f[:, 0:128],
                                rhs=E16f[:, 0:128], start=True, stop=True), s_t)
    release_bank(ps[1], s_t)
    wait("v", s_t)
    ins("v", lambda e: e.tensor_copy(out=jnk[:], in_=E16f[:, 0:128]))
    ins("v", lambda e: e.tensor_copy(out=decay_all[:], in_=ps[0][0:128, 0:16]), s_v)
    release_bank(ps[0], s_v)
    barrier()
    init_es.close()

    p0 = {"f32": [sb(f"p0f{i}", [128, 512], F32) for i in range(2)],
          "bf": [sb(f"p0b{i}", [128, 512], BF16) for i in range(2)],
          "i": 0, "tog": 0, "fin": {}, "guard": {0: 0, 1: 0},
          "cguard": {0: None, 1: None}}

    def p0_block(src_ap, dblk):
        i = p0["i"] % 2
        p0["i"] += 1
        f32b, bfb = p0["f32"][i], p0["bf"][i]
        if p0["cguard"][i] is not None:
            wait("s", p0["cguard"][i][0], p0["cguard"][i][1])
        ins("s", lambda e, o=f32b, a=src_ap: e.dma_start(out=o[:], in_=a), s_ldp[i], 16)
        ce = "v" if (p0["tog"] % 2 == 0) else "s"
        p0["tog"] += 1
        wait(ce, s_ldp[i])
        wait(ce, s_p0[i], p0["guard"][i])
        if ce == "v":
            ins("v", lambda e, o=bfb, a=f32b: e.tensor_copy(out=o[:], in_=a[:]), s_v)
            p0["cguard"][i] = (s_v, s_v.n)
            wait("g", s_v)
        else:
            ins("s", lambda e, o=bfb, a=f32b: e.copy(out=o[:], in_=a[:]), s_s)
            p0["cguard"][i] = (s_s, s_s.n)
            wait("g", s_s)
        ins("g", lambda e, d=dblk, b=bfb: e.dma_start(out=d, in_=b[:]), s_p0[i], 16)
        p0["guard"][i] = s_p0[i].n

    def p0_wlike(name, W, scratch, mg_lo, mg_hi):
        for mg in range(mg_lo, mg_hi):
            for k in range(NK):
                src2 = W[k * 128:(k + 1) * 128, mg * 512:(mg + 1) * 512]
                p0_block(src2, scratch.ap()[mg * 16 + k])

    def p0_wd(s_lo, s_hi):
        for s4 in range(s_lo, s_hi):
            for mg in range(4):
                for ci in range(16):
                    c = 16 * s4 + ci
                    j, m = c // 4, c % 4
                    src2 = w_down[512 * j + m: 512 * j + m + 509: 4, mg * 512:(mg + 1) * 512]
                    idx = s4 * 64 + mg * 16 + ci
                    p0_block(src2, wd_bf.ap()[idx])

    def p0_fin(name):
        p0["fin"][name] = (s_p0[0].n, s_p0[1].n)

    def ln_rows_and_bcast(store_isrow=None, store_misrow=None):
        tv = s_t.n
        wait("v", s_t, tv)
        inv_n = 1.0 / D
        mu = rows[0:1, 0:512]
        e2 = rows[0:1, 512:1024]
        var = rows[0:1, 1024:1536]
        mis = rows[0:1, 1536:2048]
        ins("v", lambda e: e.tensor_single_scalar(out=mu, in_=ps_st[0:1, 0:512], scalar=inv_n, op=OP.mult))
        ins("v", lambda e: e.tensor_single_scalar(out=e2, in_=ps_st[0:1, 512:1024], scalar=inv_n, op=OP.mult))
        ins("v", lambda e: e.tensor_tensor(out=var, in0=mu, in1=mu, op=OP.mult))
        ins("v", lambda e: e.tensor_tensor(out=var, in0=e2, in1=var, op=OP.subtract))
        ins("v", lambda e: e.tensor_single_scalar(out=var, in_=var, scalar=EPS_LN, op=OP.add), s_v)
        release_bank(ps_st, s_v)
        wait("s", s_v)
        ins("s", lambda e: e.activation(out=e2, in_=var, func=AF.Ln), s_s)
        isdst = bc_stat[0:1, 0:512] if store_isrow is None else store_isrow
        wait("s", s_s)
        wait("s", s_t)
        ins("s", lambda e, d=isdst: e.activation(out=d, in_=e2, func=AF.Exp, scale=-0.5), s_s)
        wait("v", s_s)
        wait("v", s_gd)
        ins("v", lambda e, d=isdst: e.tensor_tensor(out=mis, in0=mu, in1=d, op=OP.mult), s_v)
        wait("g", s_v)
        misdst = bc_stat[1:2, 0:512] if store_misrow is None else store_misrow
        ins("g", lambda e, d=misdst: e.dma_start(out=d, in_=mis), s_gd, 16)
        if store_isrow is None:
            wait("t", s_gd)
            bcast_pair_from(bc_stat[:, 0:512])

    def bcast_pair_from(bc_slice):
        use_bank(ps[0])
        use_bank(ps[1])
        ins("t", lambda e, a=bc_slice: e.matmul(out=ps[0][:], lhsT=E16f[:, 0:128], rhs=a, start=True, stop=True))
        ins("t", lambda e, a=bc_slice: e.matmul(out=ps[1][:], lhsT=E16f[:, 128:256], rhs=a, start=True, stop=True), s_t)
        wait("v", s_t)
        ins("v", lambda e: e.tensor_copy(out=is_b[:], in_=ps[0][:]))
        ins("v", lambda e: e.tensor_copy(out=mis_b[:], in_=ps[1][:]), s_v)
        release_bank(ps[0], s_v)
        release_bank(ps[1], s_v)

    p0_wlike("wq", w_qkv, wq_bf, 0, 12)
    p0_fin("wq")

    a_es = ExitStack()
    xc = [sb(f"xc{i}", [128, TA], F32, a_es) for i in range(4)]
    xcb = [sb(f"xcb{i}", [128, TA], BF16, a_es) for i in range(2)]
    xcs = [sb(f"xcs{i}", [128, TA], BF16, a_es) for i in range(2)]
    bcLN1 = sb("bcLN1", [128, TOK], F32, a_es)
    ln1x2 = [sb(f"ln1x{i}", [128, NK * TA], BF16, a_es) for i in range(2)]
    qtile = sb("qtile", [128, H * TA], BF16, a_es)
    ktile = sb("ktile", [128, H * TA], BF16, a_es)
    Abuf = [sb(f"Abuf{i}", [128, TA], BF16, a_es) for i in range(2)]
    prodb = sb("prodb", [128, H * TA], BF16, a_es)
    kvt = sb("kvt", [128, TA + 1], F32, a_es)
    kvloc = sb("kvloc", [128, TA + 1], F32, a_es)
    ksloc = sb("ksloc", [128, TA + 1], F32, a_es)
    phs = sb("phs", [128, TA + 1], F32, a_es)
    ksin = phs
    tmpa = sb("tmpa", [128, TA], F32, a_es)

    marks = {}
    ins("v", lambda e: e.memset(bcLN1[:], 0.0), s_v)

    for ta in range(NTA):
        t0 = ta * TA
        for k in range(NK):
            xck = xc[k % 4]
            gk = ("ln1c", ta, k - 4) if k >= 4 else (("ln1c", ta - 1, 12 + k) if ta > 0 else None)
            if gk is not None and gk in marks:
                wait("y", s_v, marks[gk][0])
                wait("y", s_s, marks[gk][1])
            xin = x_fm[k * 128:(k + 1) * 128, t0:t0 + TA]
            ins("y", lambda e, a=xin, o=xck: e.dma_start(out=o[:], in_=a), s_l2[k % 4], 16)
            cb, sq = xcb[k % 2], xcs[k % 2]
            gk2 = ("ln1col", ta, k - 2) if k >= 2 else (("ln1col", ta - 1, 14 + k) if ta > 0 else None)
            wait("v", s_l2[k % 4])
            wait("s", s_l2[k % 4])
            if gk2 is not None and gk2 in marks:
                wait("v", s_t, marks[gk2])
                wait("s", s_t, marks[gk2])
            ins("v", lambda e, o=cb, a=xck: e.tensor_copy(out=o[:], in_=a[:]), s_v)
            ins("s", lambda e, o=sq, a=xck: e.activation(out=o[:], in_=a[:], func=AF.Square), s_s)
            vv, sv = s_v.n, s_s.n
            marks[("ln1c", ta, k)] = (s_v.n, s_s.n)
            if k == 0:
                use_bank(ps_st)
            wait("t", s_v, vv)
            wait("t", s_s, sv)
            ins("t", lambda e, a=cb, st=(k == 0), sp=(k == NK - 1): e.matmul(
                out=ps_st[0:1, 0:512], lhsT=onesb[:], rhs=a[:], start=st, stop=sp))
            ins("t", lambda e, a=sq, st=(k == 0), sp=(k == NK - 1): e.matmul(
                out=ps_st[0:1, 512:1024], lhsT=onesb[:], rhs=a[:], start=st, stop=sp), s_t)
            marks[("ln1col", ta, k)] = s_t.n
        ln_rows_and_bcast(store_isrow=bcLN1[0:1, t0:t0 + TA], store_misrow=bcLN1[1:2, t0:t0 + TA])

    def a_apply(ta):
        t0 = ta * TA
        steps = []
        def bcast_step():
            wait("t", s_gd)
            bcast_pair_from(bcLN1[:, t0:t0 + TA])
            marks[("a_bc", ta)] = s_v.n
        steps.append(bcast_step)
        for k in range(NK):
            def chunk(k=k):
                xck = xc[k % 4]
                if k >= 4:
                    gk = ("ln1ap", ta, k - 4)
                elif ta > 0:
                    gk = ("ln1ap", ta - 1, 12 + k)
                else:
                    gk = ("ln1c", NTA - 1, 12 + k)
                if gk in marks:
                    mv = marks[gk]
                    if isinstance(mv, tuple):
                        wait("g", s_v, mv[0])
                        wait("g", s_s, mv[1])
                    else:
                        wait("g", s_v, mv)
                xin = x_fm[k * 128:(k + 1) * 128, t0:t0 + TA]
                ins("g", lambda e, a=xin, o=xck: e.dma_start(out=o[:], in_=a), s_ax[k % 4], 16)
                wait("v", s_ax[k % 4])
                if k == 0:
                    wait("v", s_v, marks[("a_bc", ta)])
                ins("v", lambda e, a=xck: e.tensor_tensor(out=tmpa[:], in0=a[:], in1=is_b[:], op=OP.mult))
                ins("v", lambda e, k=k, o=ln1x2[ta % 2]: e.tensor_tensor(
                    out=o[:, k * TA:(k + 1) * TA], in0=tmpa[:], in1=mis_b[:], op=OP.subtract), s_v)
                marks[("ln1ap", ta, k)] = s_v.n
            steps.append(chunk)
        def fin():
            marks[("ln1x_rdy", ta)] = s_v.n
        steps.append(fin)
        return steps

    for st_ in a_apply(0):
        st_()
    marks[("ln1x_rdy", 0)] = s_v.n

    wait("y", s_p0[0], p0["fin"]["wq"][0])
    wait("y", s_p0[1], p0["fin"]["wq"][1])
    for ta in range(NTA):
        t0 = ta * TA
        ln1x = ln1x2[ta % 2]
        nxt = a_apply(ta + 1) if ta + 1 < NTA else []
        nxt_i = 0
        if ta > 0:
            wait("v", s_qsp, marks[("qsp", ta - 1)])
        wait("t", s_v, marks[("ln1x_rdy", ta)])
        for mq in range(48):
            mg = mq // 4
            if mq % 4 == 0:
                grp_blocks = [ring_load(wq_bf.ap()[mg * 16 + k]) for k in range(NK)]
                wait("t", grp_blocks[-1][1][0], grp_blocks[-1][1][1])
            if mq % 3 == 0 and nxt_i < len(nxt):
                nxt[nxt_i]()
                nxt_i += 1
            pb = ps[mq % 5]
            use_bank(pb)
            for k in range(NK):
                ins("t", lambda e, w=grp_blocks[k][0], k=k, mq=mq, p=pb, lx=ln1x: e.matmul(
                    out=p[:], lhsT=w[:, (mq % 4) * 128:(mq % 4 + 1) * 128],
                    rhs=lx[:, k * TA:(k + 1) * TA],
                    start=(k == 0), stop=(k == NK - 1)),
                    s_t if k == NK - 1 else None)
            if mq % 4 == 3:
                ring_release_through(grp_blocks[-1][2], s_t)
            mm_done = s_t.n
            h = mq % 16
            if mq < 16:
                wait("v", s_t, mm_done)
                ins("v", lambda e, p=pb: e.tensor_single_scalar(out=phs[:, 0:TA], in_=p[:], scalar=0.0, op=OP.min), s_v)
                wait("s", s_v)
                ins("s", lambda e: e.activation(out=kvloc[:, 0:TA], in_=phs[:, 0:TA], func=AF.Exp), s_s)
                ins("v", lambda e, p=pb: e.tensor_single_scalar(out=tmpa[:], in_=p[:], scalar=0.0, op=OP.max), s_v)
                release_bank(pb, s_v)
                wait("v", s_s)
                ins("v", lambda e, h=h: e.tensor_tensor(out=qtile[:, h * TA:(h + 1) * TA], in0=kvloc[:, 0:TA], in1=tmpa[:], op=OP.add), s_v)
            elif mq < 32:
                wait("v", s_t, mm_done)
                ins("v", lambda e, p=pb: e.tensor_single_scalar(out=phs[:, 0:TA], in_=p[:], scalar=0.0, op=OP.min), s_v)
                wait("s", s_v)
                ins("s", lambda e: e.activation(out=kvloc[:, 0:TA], in_=phs[:, 0:TA], func=AF.Exp), s_s)
                ins("v", lambda e, p=pb: e.tensor_single_scalar(out=tmpa[:], in_=p[:], scalar=0.0, op=OP.max), s_v)
                release_bank(pb, s_v)
                wait("v", s_s)
                ins("v", lambda e, h=h: e.tensor_tensor(out=ktile[:, h * TA:(h + 1) * TA], in0=kvloc[:, 0:TA], in1=tmpa[:], op=OP.add))
                ins("v", lambda e, h=h: e.tensor_copy(out=ksin[:, 0:1], in_=stK[:, h:h + 1]))
                ins("v", lambda e, h=h: e.tensor_copy(out=ksin[:, 1:TA + 1], in_=ktile[:, h * TA:(h + 1) * TA]))
                ins("v", lambda e, h=h: e.tensor_tensor_scan(
                    out=ksloc[:], data0=decay_all[:, h:h + 1].broadcast_to([128, TA + 1]),
                    data1=ksin[:], initial=0.0, op0=OP.mult, op1=OP.add))
                ins("v", lambda e, h=h: e.tensor_tensor(out=prodb[:, h * TA:(h + 1) * TA], in0=qtile[:, h * TA:(h + 1) * TA], in1=ksloc[:, 1:TA + 1], op=OP.mult))
                ins("v", lambda e, h=h: e.tensor_copy(out=stK[:, h:h + 1], in_=ksloc[:, TA:TA + 1]), s_v)
            else:
                wait("v", s_t, mm_done)
                ins("v", lambda e, h=h: e.tensor_copy(out=kvt[:, 0:1], in_=stKV[:, h:h + 1]))
                ins("v", lambda e, p=pb, h=h: e.tensor_tensor(out=kvt[:, 1:TA + 1], in0=ktile[:, h * TA:(h + 1) * TA], in1=p[:], op=OP.mult), s_v)
                release_bank(pb, s_v)
                ins("v", lambda e, h=h: e.tensor_tensor_scan(
                    out=kvloc[:], data0=decay_all[:, h:h + 1].broadcast_to([128, TA + 1]),
                    data1=kvt[:], initial=0.0, op0=OP.mult, op1=OP.add))
                ab = Abuf[h % 2]
                gk = ("Asp", ta, h - 2) if h >= 2 else (("Asp", ta - 1, 14 + h) if ta > 0 else None)
                if gk is not None and gk in marks:
                    wait("v", s_sp[h % 2], marks[gk])
                ins("v", lambda e, h=h, ab=ab: e.tensor_tensor(out=ab[:], in0=qtile[:, h * TA:(h + 1) * TA], in1=kvloc[:, 1:TA + 1], op=OP.mult))
                ins("v", lambda e, h=h: e.tensor_copy(out=stKV[:, h:h + 1], in_=kvloc[:, TA:TA + 1]), s_v)
                wait("g", s_v)
                ins("g", lambda e, h=h, ab=ab, ta=ta: e.dma_start(out=A_sp.ap()[ta][:, h * TA:(h + 1) * TA], in_=ab[:]), s_sp[h % 2], 16)
                marks[("Asp", ta, h)] = s_sp[h % 2].n
        while nxt_i < len(nxt):
            nxt[nxt_i]()
            nxt_i += 1
        if ta + 1 < NTA:
            marks[("ln1x_rdy", ta + 1)] = s_v.n
        use_bank(ps_den)
        wait("t", s_v)
        for h in range(16):
            ins("t", lambda e, h=h: e.matmul(
                out=ps_den[:], lhsT=P16b[:, h * 16:(h + 1) * 16], rhs=prodb[:, h * TA:(h + 1) * TA],
                start=(h == 0), stop=(h == 15)), s_t if h == 15 else None)
        marks[("denmm", ta)] = s_t.n
        wait("v", s_t)
        ins("v", lambda e, t0=t0: e.tensor_copy(out=den_loc[:, t0:t0 + TA], in_=ps_den[:]), s_v)
        release_bank(ps_den, s_v)
        wait("g", s_v)
        ins("g", lambda e, ta=ta: e.dma_start(out=q_sp.ap()[ta], in_=qtile[:]), s_qsp, 16)
        marks[("qsp", ta)] = s_qsp.n
        if ta == 0:
            p0_wlike("wo", w_out, wo_bf, 0, 4)
            p0_fin("wo")
        elif ta == 1:
            p0_wlike("wg1", w_gate, wg_bf, 0, 16)
        elif ta == 2:
            p0_wlike("wu1", w_up, wu_bf, 0, 16)
        elif ta == 3:
            p0_wd(0, 4)
            p0_fin("ffn")

    barrier()
    ins("g", lambda e: e.dma_start(out=cc_in.ap()[:, 0:16], in_=stKV[:]), s_gd, 16)
    ins("g", lambda e: e.dma_start(out=cc_in.ap()[:, 16:32], in_=stK[:]), s_gd, 16)
    wait("g", s_gd)
    ins("g", lambda e: e.collective_compute(
        "AllGather", OP.bypass, replica_groups=[[0, 1, 2, 3], [4, 5, 6, 7]],
        ins=[cc_in.ap().opt()], outs=[cc_out.ap().opt()]), s_cc, 1)
    wait("g", s_cc)
    for r in range(4):
        ins("g", lambda e, r=r: e.dma_start(out=G[:, r * 32:(r + 1) * 32], in_=cc_out.ap()[r]), s_gd, 16)
    gv = s_gd.n
    wait("s", s_gd, gv)
    ins("s", lambda e: e.activation(out=wm16[:], in_=garrT[:], func=AF.Exp, scale=lnac[:]), s_s)
    wait("v", s_s)
    ins("v", lambda e: e.tensor_tensor(out=wm16[:], in0=wm16[:], in1=maskT[:], op=OP.mult), s_v)
    ins("v", lambda e: e.memset(wm4[:], 0.0), s_v)
    wait("g", s_v)
    for r in range(4):
        ins("g", lambda e, r=r: e.dma_start(out=wm4[r:r + 1, 0:16], in_=wm16[0:16, r:r + 1]), s_wm, 16)
    wait("t", s_wm)
    use_bank(ps[2])
    use_bank(ps[3])
    for r in range(4):
        ins("t", lambda e, r=r: e.matmul(out=ps[2][0:128, r * 16:(r + 1) * 16],
                                         lhsT=E16f[:, r * 128:(r + 1) * 128],
                                         rhs=wm4[:], start=True, stop=True))
    ins("t", lambda e: e.matmul(out=ps[3][:, 0:128], lhsT=E16f[:, 0:128],
                                rhs=E16f[:, 0:128], start=True, stop=True), s_t)
    release_bank(ps[3], s_t)
    wait("v", s_t)
    ins("v", lambda e: e.tensor_copy(out=jnk[:], in_=E16f[:, 0:128]))
    for r in range(4):
        if r == 0:
            ins("v", lambda e, r=r: e.tensor_tensor(out=skv_in[:], in0=G[:, r * 32:r * 32 + 16], in1=ps[2][0:128, r * 16:r * 16 + 16], op=OP.mult))
            ins("v", lambda e, r=r: e.tensor_tensor(out=sk_in[:], in0=G[:, r * 32 + 16:r * 32 + 32], in1=ps[2][0:128, r * 16:r * 16 + 16], op=OP.mult))
        else:
            ins("v", lambda e, r=r: e.tensor_tensor(out=sk_tmp[:], in0=G[:, r * 32:r * 32 + 16], in1=ps[2][0:128, r * 16:r * 16 + 16], op=OP.mult))
            ins("v", lambda e: e.tensor_tensor(out=skv_in[:], in0=skv_in[:], in1=sk_tmp[:], op=OP.add))
            ins("v", lambda e, r=r: e.tensor_tensor(out=sk_tmp[:], in0=G[:, r * 32 + 16:r * 32 + 32], in1=ps[2][0:128, r * 16:r * 16 + 16], op=OP.mult))
            ins("v", lambda e: e.tensor_tensor(out=sk_in[:], in0=sk_in[:], in1=sk_tmp[:], op=OP.add))
    ins("v", lambda e: e.engine_nop(), s_v)
    release_bank(ps[2], s_v)
    ins("v", lambda e: e.memset(kspad[:], 0.0))
    ins("v", lambda e: e.tensor_copy(out=kspad[:, 0:16], in_=sk_in[:]))
    for h in range(15, 0, -1):
        ins("v", lambda e, h=h: e.tensor_copy(out=kspad[:, 16 * h + h:16 * h + h + 1], in_=kspad[:, h:h + 1]))
    ins("v", lambda e: e.memset(kspad[:, 1:16], 0.0), s_v)
    wait("g", s_v)
    for h in range(16):
        ins("g", lambda e, h=h: e.dma_start(out=G[h:h + 1, 0:128], in_=skv_in[:, h:h + 1],
                                            allow_slow_non_contiguous=True), s_wm, 16)
    wait("v", s_wm)
    for h in range(16):
        ins("v", lambda e, h=h: e.tensor_tensor(out=rankstb[0:16, h * 128:(h + 1) * 128],
                                                in0=G[0:16, 0:128], in1=E16f[0:16, h * 128:(h + 1) * 128], op=OP.mult),
            s_v if h == 15 else None)
    barrier()
    a_es.close()

    c1_es = ExitStack()
    qtc = sb("qtc", [128, H * TC], BF16, c1_es)
    Atc = sb("Atc", [128, H * TC], BF16, c1_es)
    attnt2 = [sb(f"attnt{i}", [128, H * TC], BF16, c1_es) for i in range(2)]
    yt = sb("yt", [128, NK * TC], F32, c1_es)
    ybc = [sb(f"ybc{i}", [128, TC], BF16, c1_es) for i in range(2)]
    ysq = [sb(f"ysq{i}", [128, TC], BF16, c1_es) for i in range(2)]
    x2bc = [sb(f"x2bc{i}", [128, TC], BF16, c1_es) for i in range(2)]
    x2sq = [sb(f"x2sq{i}", [128, TC], BF16, c1_es) for i in range(2)]
    dent = sb("dent", [16, TC], F32, c1_es)
    r16b = sb("r16b", [128, TC], BF16, c1_es)
    tmpc = sb("tmpc", [128, TC], F32, c1_es)

    wait("y", s_p0[0], p0["fin"]["wo"][0])
    wait("y", s_p0[1], p0["fin"]["wo"][1])
    ins("v", lambda e: e.memset(r16b[:], 0.0), s_v)

    def c1_head(tc):
        t0 = tc * TC
        attnt = attnt2[tc % 2]
        if tc > 0:
            wait("y", s_v, marks[("c1_free", tc - 1)])
        ins("y", lambda e, tc=tc: e.dma_start(out=qtc[:], in_=q_sp.ap()[tc]), s_qa, 16)
        ins("y", lambda e, tc=tc: e.dma_start(out=Atc[:], in_=A_sp.ap()[tc]), s_qa, 16)
        qa_ready = s_qa.n
        use_bank(ps_den)
        wait("t", s_qa, qa_ready)
        for h in range(16):
            ins("t", lambda e, h=h: e.matmul(out=ps_den[:], lhsT=kspad[:, h * 16:(h + 1) * 16],
                                             rhs=qtc[:, h * TC:(h + 1) * TC], start=(h == 0), stop=(h == 15)),
                s_t if h == 15 else None)
        wait("v", s_t)
        ins("v", lambda e, t0=t0: e.tensor_tensor(out=dent[:], in0=c16[:, t0:t0 + TC], in1=ps_den[:], op=OP.mult), s_v)
        release_bank(ps_den, s_v)
        ins("v", lambda e, t0=t0: e.tensor_tensor(out=dent[:], in0=dent[:], in1=den_loc[:, t0:t0 + TC], op=OP.add))
        ins("v", lambda e: e.tensor_single_scalar(out=dent[:], in_=dent[:], scalar=EPS_DEN, op=OP.max))
        ins("v", lambda e: e.reciprocal(out=dent[:], in_=dent[:]))
        ins("v", lambda e: e.tensor_copy(out=r16b[0:16, :], in_=dent[:]), s_v)
        rdy = s_v.n
        if tc > 1:
            wait("v", s_t, marks[("wout_mm", tc - 2)])
        wait("t", s_v, rdy)
        for h in range(16):
            pc = ps[2] if h % 2 == 0 else ps[3]
            pr = ps[4]
            use_bank(pc)
            ins("t", lambda e, h=h, t0=t0, p=pc: e.matmul(
                out=p[:], lhsT=rankstb[:, h * 128:(h + 1) * 128], rhs=c16b[:, t0:t0 + TC], start=True, stop=True), s_t)
            use_bank(pr)
            ins("t", lambda e, h=h, p=pr: e.matmul(
                out=p[:], lhsT=E16b[:, h * 128:(h + 1) * 128], rhs=r16b[:], start=True, stop=True), s_t)
            mmv = s_t.n
            wait("v", s_t, mmv)
            ins("v", lambda e, h=h, p=pc: e.tensor_tensor(out=tmpc[:], in0=qtc[:, h * TC:(h + 1) * TC], in1=p[:], op=OP.mult), s_v)
            release_bank(pc, s_v)
            ins("v", lambda e, h=h: e.tensor_tensor(out=tmpc[:], in0=tmpc[:], in1=Atc[:, h * TC:(h + 1) * TC], op=OP.add))
            ins("v", lambda e, h=h, a=attnt, p=pr: e.tensor_tensor(out=a[:, h * TC:(h + 1) * TC], in0=tmpc[:], in1=p[:], op=OP.mult), s_v)
            release_bank(pr, s_v)
        marks[("c1_free", tc)] = s_v.n
        marks[("attn_rdy", tc)] = s_v.n

    def c1_wout(tc):
        attnt = attnt2[tc % 2]
        wait("t", s_v, marks[("attn_rdy", tc)])
        for mo in range(16):
            mg = mo // 4
            if mo % 4 == 0:
                grp_blocks = [ring_load(wo_bf.ap()[mg * 16 + k]) for k in range(NK)]
                wait("t", grp_blocks[-1][1][0], grp_blocks[-1][1][1])
            pb = ps[mo % 2]
            use_bank(pb)
            for k in range(NK):
                ins("t", lambda e, w=grp_blocks[k][0], k=k, mo=mo, p=pb: e.matmul(
                    out=p[:], lhsT=w[:, (mo % 4) * 128:(mo % 4 + 1) * 128],
                    rhs=attnt[:, k * TC:(k + 1) * TC], start=(k == 0), stop=(k == NK - 1)),
                    s_t if k == NK - 1 else None)
            if mo % 4 == 3:
                ring_release_through(grp_blocks[-1][2], s_t)
            mmv = s_t.n
            yb, sq = ybc[mo % 2], ysq[mo % 2]
            gk = ("c1col", tc, mo - 2) if mo >= 2 else (("c1col", tc - 1, 14 + mo) if tc > 0 else None)
            if gk is not None and gk in marks:
                wait("s", s_t, marks[gk])
            wait("s", s_t, mmv)
            ins("s", lambda e, mo=mo, p=pb: e.copy(out=yt[:, mo * TC:(mo + 1) * TC], in_=p[:]))
            ins("s", lambda e, p=pb, o=yb: e.copy(out=o[:], in_=p[:]))
            ins("s", lambda e, p=pb, o=sq: e.activation(out=o[:], in_=p[:], func=AF.Square), s_s)
            release_bank(pb, s_s)
            sv = s_s.n
            if mo == 0:
                use_bank(ps_st)
            wait("t", s_s, sv)
            ins("t", lambda e, a=yb, st=(mo == 0), sp=(mo == 15): e.matmul(
                out=ps_st[0:1, 0:512], lhsT=onesb[:], rhs=a[:], start=st, stop=sp))
            ins("t", lambda e, a=sq, st=(mo == 0), sp=(mo == 15): e.matmul(
                out=ps_st[0:1, 512:1024], lhsT=onesb[:], rhs=a[:], start=st, stop=sp), s_t)
            marks[("c1col", tc, mo)] = s_t.n
        marks[("wout_mm", tc)] = s_t.n

    def c1_tail(tc):
        t0 = tc * TC
        ln_rows_and_bcast()
        wait("v", s_v)
        for k in range(NK):
            ins("v", lambda e, k=k: e.tensor_tensor(out=tmpc[:], in0=yt[:, k * TC:(k + 1) * TC], in1=is_b[:], op=OP.mult))
            ins("v", lambda e, k=k: e.tensor_tensor(out=yt[:, k * TC:(k + 1) * TC], in0=tmpc[:], in1=mis_b[:], op=OP.subtract),
                s_v if k == NK - 1 else None)
        wait("g", s_v)
        for k in range(NK):
            xin = x_fm[k * 128:(k + 1) * 128, t0:t0 + TC]
            ins("g", lambda e, k=k, a=xin: e.dma_start(out=yt[:, k * TC:(k + 1) * TC], in_=a, accum_op=OP.add), s_ac, 16)
        acc_done = s_ac.n
        wait("s", s_ac, acc_done)
        wait("v", s_ac, acc_done)
        for k in range(NK):
            xb2, sq2 = x2bc[k % 2], x2sq[k % 2]
            gk = ("ln2col", tc, k - 2) if k >= 2 else (("ln2col", tc - 1, 14 + k) if tc > 0 else None)
            if gk is not None and gk in marks:
                wait("s", s_t, marks[gk])
                wait("v", s_t, marks[gk])
            gk2 = ("x2spill", tc, k - 2) if k >= 2 else (("x2spill", tc - 1, 14 + k) if tc > 0 else None)
            if gk2 is not None and gk2 in marks:
                wait("s", s_x2s[k % 2], marks[gk2])
            ins("s", lambda e, k=k, o=xb2: e.copy(out=o[:], in_=yt[:, k * TC:(k + 1) * TC]), s_s)
            wait("v", s_s)
            ins("v", lambda e, o=sq2, a=xb2: e.tensor_tensor(out=o[:], in0=a[:], in1=a[:], op=OP.mult), s_v)
            sv, vv = s_s.n, s_v.n
            wait("g", s_s, sv)
            ins("g", lambda e, k=k, a=xb2, tc=tc: e.dma_start(out=x2_sp.ap()[k][:, tc * TC:(tc + 1) * TC], in_=a[:]), s_x2s[k % 2], 16)
            marks[("x2spill", tc, k)] = s_x2s[k % 2].n
            if k == 0:
                use_bank(ps_st)
            wait("t", s_s, sv)
            wait("t", s_v, vv)
            ins("t", lambda e, a=xb2, st=(k == 0), sp=(k == NK - 1): e.matmul(
                out=ps_st[0:1, 0:512], lhsT=onesb[:], rhs=a[:], start=st, stop=sp))
            ins("t", lambda e, a=sq2, st=(k == 0), sp=(k == NK - 1): e.matmul(
                out=ps_st[0:1, 512:1024], lhsT=onesb[:], rhs=a[:], start=st, stop=sp), s_t)
            marks[("ln2col", tc, k)] = s_t.n
        ln_rows_and_bcast(store_isrow=bcLN2[0:1, t0:t0 + TC], store_misrow=bcLN2[1:2, t0:t0 + TC])

    c1_head(0)
    for tc in range(NTC):
        c1_wout(tc)
        if tc + 1 < NTC:
            c1_head(tc + 1)
        c1_tail(tc)

    barrier()
    c1_es.close()

    c2_es = ExitStack()
    wring2 = [sb(f"wr2_{i}", [128, 512], BF16, c2_es) for i in range(14)]
    ring["slots"] = wring + wring2
    ring["issued"] = 0
    ring["marks"] = [(10 ** 9, s_t, s_t.n)]
    ring["marks"] = []
    ln2x = sb("ln2x", [128, NK * T2], BF16, c2_es)
    xv2 = [sb(f"xv2_{i}", [128, T2], BF16, c2_es) for i in range(4)]
    hslab = sb("hslab", [128, 16 * T2], BF16, c2_es)
    facc = sb("facc", [128, NK * T2], F32, c2_es)
    hm = [sb(f"hm{i}", [128, T2], F32, c2_es) for i in range(4)]
    gs = [sb(f"gs{i}", [128, T2], F32, c2_es) for i in range(4)]
    m2t = sb("m2t", [128, T2], F32, c2_es)
    m2u = sb("m2u", [128, T2], F32, c2_es)
    yb3 = [sb(f"yb3_{i}", [128, T2], BF16, c2_es) for i in range(2)]
    ys3 = [sb(f"ys3_{i}", [128, T2], BF16, c2_es) for i in range(2)]
    obuf = [sb(f"obuf{i}", [128, T2], F32, c2_es) for i in range(2)]
    tmp2 = sb("tmp2", [128, T2], F32, c2_es)

    wait("y", s_p0[0], p0["fin"]["ffn"][0])
    wait("y", s_p0[1], p0["fin"]["ffn"][1])
    for t2 in range(NT2):
        t0 = t2 * T2
        bcast_pair_from(bcLN2[:, t0:t0 + T2])
        bcv = s_v.n
        if t2 > 0:
            wait("v", s_t, marks[("ffn_mm", t2 - 1)])
            wait("v", s_s, marks[("ln3cast", t2 - 1)])
        for k in range(NK):
            xb2 = xv2[k % 4]
            gk = ("ln2x", t2, k - 4) if k >= 4 else (("fout_v", t2 - 1, 12 + k) if t2 > 0 else None)
            if gk is not None and gk in marks:
                wait("y", s_v, marks[gk])
            ins("y", lambda e, k=k, t0=t0, o=xb2: e.dma_start(out=o[:], in_=x2_sp.ap()[k][:, t0:t0 + T2]), s_l2[k % 4], 16)
            wait("v", s_l2[k % 4])
            if k == 0:
                wait("v", s_v, bcv)
            ins("v", lambda e, k=k, a=xb2: e.tensor_tensor(out=tmp2[:], in0=a[:], in1=is_b[:], op=OP.mult))
            ins("v", lambda e, k=k: e.tensor_tensor(out=ln2x[:, k * T2:(k + 1) * T2], in0=tmp2[:], in1=mis_b[:], op=OP.subtract), s_v)
            marks[("ln2x", t2, k)] = s_v.n
        ln2_ready = s_v.n
        for s4 in range(4):
            if s4 > 0:
                wait("v", s_t, marks[("down_mm", t2, s4 - 1)])
            for jj in range(4):
                j = 4 * s4 + jj
                if s4 + jj > 0:
                    wait("s", s_v, marks.get(("spars", t2, s4, jj - 1),
                                             marks.get(("spars", t2, s4 - 1, 3), 0)))
                gblocks = [ring_load(wg_bf.ap()[j * 16 + k]) for k in range(NK)]
                wait("t", gblocks[-1][1][0], gblocks[-1][1][1])
                for m in range(4):
                    pg = ps[[2, 3, 4, 0][m]]
                    use_bank(pg)
                    for k in range(NK):
                        if s4 == 0 and jj == 0 and m == 0:
                            wait("t", s_v, marks[("ln2x", t2, k)])
                        ins("t", lambda e, w=gblocks[k][0], k=k, m=m, p=pg: e.matmul(
                            out=p[:], lhsT=w[:, m::4], rhs=ln2x[:, k * T2:(k + 1) * T2],
                            start=(k == 0), stop=(k == NK - 1)), s_t if k == NK - 1 else None)
                    gv2 = s_t.n
                    wait("s", s_t, gv2)
                    ins("s", lambda e, m=m, p=pg: e.activation(out=gs[m][:], in_=p[:], func=AF.Silu), s_s)
                    release_bank(pg, s_s)
                ring_release_through(gblocks[-1][2], s_t)
                ublocks = [ring_load(wu_bf.ap()[j * 16 + k]) for k in range(NK)]
                wait("t", ublocks[-1][1][0], ublocks[-1][1][1])
                for m in range(4):
                    pu = ps[[2, 3, 4, 0][m]]
                    use_bank(pu)
                    for k in range(NK):
                        ins("t", lambda e, w=ublocks[k][0], k=k, m=m, p=pu: e.matmul(
                            out=p[:], lhsT=w[:, m::4], rhs=ln2x[:, k * T2:(k + 1) * T2],
                            start=(k == 0), stop=(k == NK - 1)), s_t if k == NK - 1 else None)
                    uv = s_t.n
                    wait("v", s_t, uv)
                    wait("v", s_s)
                    ins("v", lambda e, m=m, p=pu: e.tensor_tensor(out=hm[m][:], in0=gs[m][:], in1=p[:], op=OP.mult), s_v)
                    release_bank(pu, s_v)
                ring_release_through(ublocks[-1][2], s_t)
                hm_done = s_v.n
                wait("s", s_v, hm_done)
                for m in range(4):
                    ins("s", lambda e, m=m: e.activation(out=gs[m][:], in_=hm[m][:], func=AF.Square),
                        s_s if m == 3 else None)
                wait("v", s_s)
                ins("v", lambda e: e.tensor_tensor(out=m2t[:], in0=gs[0][:], in1=gs[1][:], op=OP.max))
                ins("v", lambda e: e.tensor_tensor(out=m2u[:], in0=gs[2][:], in1=gs[3][:], op=OP.max))
                ins("v", lambda e: e.tensor_tensor(out=m2t[:], in0=m2t[:], in1=m2u[:], op=OP.min))
                ins("v", lambda e: e.tensor_tensor(out=m2u[:], in0=gs[0][:], in1=gs[1][:], op=OP.min))
                ins("v", lambda e: e.tensor_tensor(out=tmp2[:], in0=gs[2][:], in1=gs[3][:], op=OP.min))
                ins("v", lambda e: e.tensor_tensor(out=m2u[:], in0=m2u[:], in1=tmp2[:], op=OP.max))
                ins("v", lambda e: e.tensor_tensor(out=m2t[:], in0=m2t[:], in1=m2u[:], op=OP.max))
                for m in range(4):
                    ins("v", lambda e, m=m: e.tensor_tensor(out=tmp2[:], in0=gs[m][:], in1=m2t[:], op=OP.is_ge))
                    ins("v", lambda e, m=m, jj=jj: e.tensor_tensor(
                        out=hslab[:, (4 * jj + m) * T2:(4 * jj + m + 1) * T2], in0=tmp2[:], in1=hm[m][:], op=OP.mult),
                        s_v if m == 3 else None)
                marks[("spars", t2, s4, jj)] = s_v.n
            hs_ready = s_v.n
            wait("t", s_v, hs_ready)
            for mg in range(4):
                dblocks = [ring_load(wd_bf.ap()[s4 * 64 + mg * 16 + ci]) for ci in range(16)]
                wait("t", dblocks[-1][1][0], dblocks[-1][1][1])
                for mo in range(4 * mg, 4 * mg + 4):
                    pd = ps[[2, 3, 4, 0][mo % 4]]
                    use_bank(pd)
                    for ci in range(16):
                        ins("t", lambda e, w=dblocks[ci][0], ci=ci, mo=mo, mg=mg, p=pd: e.matmul(
                            out=p[:], lhsT=w[:, (mo - 4 * mg) * 128:(mo - 4 * mg + 1) * 128],
                            rhs=hslab[:, ci * T2:(ci + 1) * T2], start=(ci == 0), stop=(ci == 15)),
                            s_t if ci == 15 else None)
                    dv = s_t.n
                    wait("v", s_t, dv)
                    if s4 == 0:
                        ins("v", lambda e, mo=mo, p=pd: e.tensor_copy(out=facc[:, mo * T2:(mo + 1) * T2], in_=p[:]), s_v)
                    else:
                        ins("v", lambda e, mo=mo, p=pd: e.tensor_tensor(
                            out=facc[:, mo * T2:(mo + 1) * T2], in0=facc[:, mo * T2:(mo + 1) * T2], in1=p[:], op=OP.add), s_v)
                    release_bank(pd, s_v)
                ring_release_through(dblocks[-1][2], s_t)
            marks[("down_mm", t2, s4)] = s_t.n
        ffn_ready = s_v.n
        marks[("ffn_mm", t2)] = s_t.n
        wait("s", s_v, ffn_ready)
        for k in range(NK):
            yb, sqq = yb3[k % 2], ys3[k % 2]
            gk = ("ln3col", t2, k - 2) if k >= 2 else (("ln3col", t2 - 1, 14 + k) if t2 > 0 else None)
            if gk is not None and gk in marks:
                wait("s", s_t, marks[gk])
            ins("s", lambda e, k=k, o=yb: e.copy(out=o[:], in_=facc[:, k * T2:(k + 1) * T2]))
            ins("s", lambda e, k=k, o=sqq: e.activation(out=o[:], in_=facc[:, k * T2:(k + 1) * T2], func=AF.Square), s_s)
            sv = s_s.n
            if k == 0:
                use_bank(ps_st)
            wait("t", s_s, sv)
            ins("t", lambda e, a=yb, st=(k == 0), sp=(k == NK - 1): e.matmul(
                out=ps_st[0:1, 0:512], lhsT=onesb[:], rhs=a[:], start=st, stop=sp))
            ins("t", lambda e, a=sqq, st=(k == 0), sp=(k == NK - 1): e.matmul(
                out=ps_st[0:1, 512:1024], lhsT=onesb[:], rhs=a[:], start=st, stop=sp), s_t)
            marks[("ln3col", t2, k)] = s_t.n
        marks[("ln3cast", t2)] = s_s.n
        ln_rows_and_bcast()
        wait("v", s_v)
        for k in range(NK):
            xb2 = xv2[k % 4]
            ob = obuf[k % 2]
            gko = ("fout_g", t2, k - 2) if k >= 2 else (("fout_g", t2 - 1, 14 + k) if t2 > 0 else None)
            if gko is not None and gko in marks:
                wait("v", s_go[k % 2], marks[gko])
            gkx = ("fout_v", t2, k - 4) if k >= 4 else (("ln2x", t2, 12 + k))
            if gkx in marks:
                wait("y", s_v, marks[gkx])
            ins("y", lambda e, k=k, t0=t0, o=xb2: e.dma_start(out=o[:], in_=x2_sp.ap()[k][:, t0:t0 + T2]), s_fo[k % 4], 16)
            ins("v", lambda e, k=k: e.tensor_tensor(out=tmp2[:], in0=facc[:, k * T2:(k + 1) * T2], in1=is_b[:], op=OP.mult))
            ins("v", lambda e, k=k: e.tensor_tensor(out=tmp2[:], in0=tmp2[:], in1=mis_b[:], op=OP.subtract))
            wait("v", s_fo[k % 4])
            ins("v", lambda e, k=k, a=xb2, o=ob: e.tensor_tensor(out=o[:], in0=tmp2[:], in1=a[:], op=OP.add), s_v)
            marks[("fout_v", t2, k)] = s_v.n
            wait("g", s_v)
            ins("g", lambda e, k=k, t0=t0, o=ob: e.dma_start(
                out=out_fm[k * 128:(k + 1) * 128, t0:t0 + T2], in_=o[:]), s_go[k % 2], 16)
            marks[("fout_g", t2, k)] = s_go[k % 2].n

    wait("g", s_go[0])
    wait("g", s_go[1])
    wait("g", s_gd)
    c2_es.close()

    with nc.Block() as block:
        @block.sync
        def _(e):
            for f in prog["y"]:
                f(e)
        @block.tensor
        def _(e):
            for f in prog["t"]:
                f(e)
        @block.vector
        def _(e):
            for f in prog["v"]:
                f(e)
        @block.scalar
        def _(e):
            for f in prog["s"]:
                f(e)
        @block.gpsimd
        def _(e):
            for f in prog["g"]:
                f(e)
    es.close()
    import sys as _s
    print(f"[kernel] auto-fences: v={fence_count['v']} s={fence_count['s']}", file=_s.stderr)
    return nc


_NC_CACHE = {}


def kernel(L0_attn_x, L0_attn_w_qkv, L0_attn_w_out, L0_attn_decay_param,
           L0_ffn_w_gate, L0_ffn_w_up, L0_ffn_w_down):
    x = np.asarray(L0_attn_x, np.float32)
    w_qkv_a = np.ascontiguousarray(np.asarray(L0_attn_w_qkv, np.float32))
    w_out_a = np.ascontiguousarray(np.asarray(L0_attn_w_out, np.float32))
    w_gate_a = np.ascontiguousarray(np.asarray(L0_ffn_w_gate, np.float32))
    w_up_a = np.ascontiguousarray(np.asarray(L0_ffn_w_up, np.float32))
    w_down_a = np.ascontiguousarray(np.asarray(L0_ffn_w_down, np.float32))
    dec = np.ascontiguousarray(np.asarray(L0_attn_decay_param, np.float32).reshape(16, 1))

    E16a = np.zeros((128, 16 * 128), np.float32)
    for h in range(16):
        E16a[h, h * 128:(h + 1) * 128] = 1.0
    E4a = np.zeros((4, 4 * 128), np.float32)
    for r in range(4):
        E4a[r, r * 128:(r + 1) * 128] = 1.0
    P16a = np.zeros((128, 16 * 16), np.float32)
    for h in range(16):
        P16a[:, h * 16 + h] = 1.0
    AR16a = np.tile(np.arange(1, TOK + 1, dtype=np.float32), (16, 1))
    ONESa = np.ones((128, 1), np.float32)
    ID128a = np.eye(128, dtype=np.float32)

    if "nc" not in _NC_CACHE:
        _NC_CACHE["nc"] = build_program()
    nc = _NC_CACHE["nc"]

    in_maps = []
    for c in range(8):
        seq, pos = c // 4, c % 4
        x_fm_c = np.ascontiguousarray(x[seq, pos * TOK:(pos + 1) * TOK, :].T)
        garr = np.zeros((16, 4), np.float32)
        mask = np.zeros((16, 4), np.float32)
        for r in range(4):
            if r < pos:
                garr[:, r] = 2048.0 * (pos - 1 - r)
                mask[:, r] = 1.0
        in_maps.append({
            "x_fm": x_fm_c, "w_qkv": w_qkv_a, "w_out": w_out_a, "w_gate": w_gate_a,
            "w_up": w_up_a, "w_down": w_down_a, "dec16": dec, "E16": E16a, "E4": E4a,
            "P16": P16a, "AR16": AR16a, "GARR": garr, "MASK4": mask,
            "ONES": ONESa, "ID128": ID128a,
        })
    res = run_bass_kernel_spmd(nc, in_maps, core_ids=list(range(8)))
    out = np.empty((B, T, D), np.float32)
    for c in range(8):
        seq, pos = c // 4, c % 4
        out[seq, pos * TOK:(pos + 1) * TOK, :] = res.results[c]["out_fm"].T
    return out


# version 25 (final)
# speedup vs baseline: 1.0271x; 1.0238x over previous
import numpy as np
from contextlib import ExitStack

import concourse.bass as bass
import concourse.mybir as mybir
from concourse.bass_utils import run_bass_kernel_spmd

F32 = mybir.dt.float32
BF16 = mybir.dt.bfloat16
OP = mybir.AluOpType
AF = mybir.ActivationFunctionType

B, T, D = 2, 8192, 2048
H = 16
NK = D // 128
QKV = 3 * D
I = 8192
TOK = 2048
TA = 512
NTA = TOK // TA
TC = 512
NTC = TOK // TC
T2 = 512
NT2 = TOK // T2
NSLOT = 24
EPS_LN = 1e-5
EPS_DEN = 1e-6

ENGS = ("t", "v", "s", "g", "y")


def build_program():
    nc = bass.Bass("TRN2", target_bir_lowering=False, debug=False)

    x_fm = nc.dram_tensor("x_fm", [D, TOK], F32, kind="ExternalInput").ap()
    w_qkv = nc.dram_tensor("w_qkv", [D, QKV], F32, kind="ExternalInput").ap()
    w_out = nc.dram_tensor("w_out", [D, D], F32, kind="ExternalInput").ap()
    w_gate = nc.dram_tensor("w_gate", [D, I], F32, kind="ExternalInput").ap()
    w_up = nc.dram_tensor("w_up", [D, I], F32, kind="ExternalInput").ap()
    w_down = nc.dram_tensor("w_down", [I, D], F32, kind="ExternalInput").ap()
    dec16 = nc.dram_tensor("dec16", [16, 1], F32, kind="ExternalInput").ap()
    E16 = nc.dram_tensor("E16", [128, 16 * 128], F32, kind="ExternalInput").ap()
    E4 = nc.dram_tensor("E4", [4, 4 * 128], F32, kind="ExternalInput").ap()
    P16 = nc.dram_tensor("P16", [128, 16 * 16], F32, kind="ExternalInput").ap()
    AR16 = nc.dram_tensor("AR16", [16, TOK], F32, kind="ExternalInput").ap()
    GARR = nc.dram_tensor("GARR", [16, 4], F32, kind="ExternalInput").ap()
    MASK4 = nc.dram_tensor("MASK4", [16, 4], F32, kind="ExternalInput").ap()
    ONES = nc.dram_tensor("ONES", [128, 1], F32, kind="ExternalInput").ap()
    ID128 = nc.dram_tensor("ID128", [128, 128], F32, kind="ExternalInput").ap()
    out_fm = nc.dram_tensor("out_fm", [D, TOK], F32, kind="ExternalOutput").ap()

    wq_bf = nc.dram_tensor("wq_bf", [12 * 16, 128, 512], BF16)
    wo_bf = nc.dram_tensor("wo_bf", [4 * 16, 128, 512], BF16)
    wg_bf = nc.dram_tensor("wg_bf", [16 * 16, 128, 512], BF16)
    wu_bf = nc.dram_tensor("wu_bf", [16 * 16, 128, 512], BF16)
    wd_bf = nc.dram_tensor("wd_bf", [4 * 4 * 16, 128, 512], BF16)
    q_sp = nc.dram_tensor("q_sp", [NTA, 128, H * TA], BF16)
    A_sp = nc.dram_tensor("A_sp", [NTA, 128, H * TA], BF16)
    x2_sp = nc.dram_tensor("x2_sp", [NK, 128, TOK], BF16)
    cc_in = nc.dram_tensor("cc_in", [128, 32], F32)
    cc_out = nc.dram_tensor("cc_out", [4, 128, 32], F32)

    es = ExitStack()

    class Sem:
        __slots__ = ("h", "n", "name")
        def __init__(self, name):
            self.h = es.enter_context(nc.semaphore(name))
            self.n = 0
            self.name = name
    s_wl = [Sem(f"wl{i}") for i in range(4)]
    s_ld = Sem("ld")
    s_ldp = [Sem("ldpa"), Sem("ldpb")]
    s_x = Sem("x")
    s_qa = Sem("qa")
    s_l2 = [Sem(f"l2{i}") for i in range(4)]
    s_ax = [Sem(f"ax{i}") for i in range(4)]
    s_fo = [Sem(f"fo{i}") for i in range(4)]
    s_gd = Sem("gd")
    s_p0 = [Sem("p0a"), Sem("p0b")]
    s_sp = [Sem("spa"), Sem("spb")]
    s_qsp = Sem("qsp")
    s_ac = Sem("ac")
    s_x2s = [Sem("x2sa"), Sem("x2sb")]
    s_go = [Sem("goa"), Sem("gob")]
    s_wm = Sem("wm")
    s_g = Sem("g")
    s_t = Sem("t")
    s_v = Sem("v")
    s_s = Sem("s")
    s_cc = Sem("cc")
    s_bar = Sem("bar")

    prog = {e: [] for e in ENGS}

    class _FakeInstr:
        def then_inc(self, *a, **k):
            return self

    class _Rec:
        __slots__ = ("calls",)
        def __init__(self):
            self.calls = []
        def __getattr__(self, name):
            def m(*a, **kw):
                self.calls.append((name, a, kw))
                return _FakeInstr()
            return m

    _SKIP_OPS = {"dma_start", "dma_start_transpose", "wait_ge", "sem_inc",
                 "engine_nop", "nop", "drain", "collective_compute", "print"}
    _WRITE_KW = {"out", "ap", "accum_out", "out_max"}

    def _span(ap):
        try:
            off = ap.offset
            total = 0
            for st, ct in ap.ap:
                total += abs(int(st)) * (int(ct) - 1)
            return (ap.tensor.name, int(off), int(off) + total + 1)
        except Exception:
            return None

    def _foot(callrec):
        import concourse.bass as _b
        name, a, kw = callrec
        rd, wr = [], []
        if name in _SKIP_OPS:
            return rd, wr, False
        argmap = dict(kw)
        for i, v in enumerate(a):
            argmap[f"_pos{i}"] = v
        for k2, v in argmap.items():
            if hasattr(v, "tensor") and hasattr(v, "offset"):
                s = _span(v)
                if s is None:
                    continue
                if k2 in _WRITE_KW or (k2 == "_pos0" and name in ("memset", "memzero")):
                    wr.append(s)
                else:
                    rd.append(s)
        return rd, wr, True

    _win = {e: [] for e in ENGS}
    fence_count = {e: 0 for e in ENGS}

    def _conflict(rd, wr, win):
        for (prd, pwr) in win:
            for w in wr:
                for p in pwr + prd:
                    if w[0] == p[0] and w[1] < p[2] and p[1] < w[2]:
                        return True
            for r in rd:
                for p in pwr:
                    if r[0] == p[0] and r[1] < p[2] and p[1] < r[2]:
                        return True
        return False

    def ins(e, fn, sem=None, k=1):
        if e in ("v", "s", "g"):
            rec = _Rec()
            try:
                fn(rec)
            except Exception:
                rec.calls = []
            if rec.calls:
                rd, wr, tracked = _foot(rec.calls[0])
                if tracked:
                    if _conflict(rd, wr, _win[e]):
                        prog[e].append(lambda eng: eng.drain())
                        fence_count[e] += 1
                        _win[e] = []
                    _win[e].append((rd, wr))
        if sem is None:
            prog[e].append(fn)
        else:
            prog[e].append(lambda eng, f=fn, s=sem.h, kk=k: f(eng).then_inc(s, kk))
            sem.n += k

    def wait(e, sem, val=None):
        v = sem.n if val is None else val
        if v > 0:
            prog[e].append(lambda eng, s=sem.h, vv=v: eng.wait_ge(s, vv))

    def barrier():
        for sem in (s_ld, s_x, s_qa, *s_l2, *s_fo, s_gd, *s_p0, *s_sp, s_qsp,
                    s_ac, *s_x2s, *s_go, s_wm, *s_wl, *s_ldp, *s_ax, s_t, s_v, s_s, s_g):
            wait("g", sem)
        prog["g"].append(lambda eng, s=s_bar.h: eng.sem_inc(s, 1))
        s_bar.n += 1
        for e in ENGS:
            wait(e, s_bar)

    def sb(name, shape, dtype, stack=es):
        return stack.enter_context(nc.sbuf_tensor(name, shape, dtype))

    ps = [es.enter_context(nc.psum_tensor(f"ps{i}", [128, 512], F32)) for i in range(5)]
    ps_st = es.enter_context(nc.psum_tensor("ps_st", [1, 1024], F32))
    ps_den = es.enter_context(nc.psum_tensor("ps_den", [16, 512], F32))

    bank_rel = {}
    def use_bank(b):
        r = bank_rel.get(id(b))
        if r is not None:
            wait("t", r[0], r[1])
    def release_bank(b, sem):
        bank_rel[id(b)] = (sem, sem.n)

    E16f = sb("E16f", [128, 16 * 128], F32)
    E16b = sb("E16b", [128, 16 * 128], BF16)
    E4f = sb("E4f", [4, 4 * 128], F32)
    P16b = sb("P16b", [128, 16 * 16], BF16)
    onesb = sb("onesb", [128, 1], BF16)
    id128f = sb("id128f", [128, 128], F32)
    garrT = sb("garrT", [16, 4], F32)
    maskT = sb("maskT", [16, 4], F32)
    decs = sb("decs", [16, 1], F32)
    scol = sb("scol", [128, 1], F32)
    lnac = sb("lnac", [16, 1], F32)
    c16 = sb("c16", [16, TOK], F32)
    c16b = sb("c16b", [128, TOK], BF16)
    decay_all = sb("decay_all", [128, 16], F32)
    stKV = sb("stKV", [128, 16], F32)
    stK = sb("stK", [128, 16], F32)
    den_loc = sb("den_loc", [16, TOK], F32)
    rankstb = sb("rankstb", [128, 16 * 128], BF16)
    kspad = sb("kspad", [128, 16 * 16], BF16)
    bcLN2 = sb("bcLN2", [128, TOK], F32)
    bc_stat = sb("bc_stat", [128, 512], F32)
    is_b = sb("is_b", [128, 512], F32)
    mis_b = sb("mis_b", [128, 512], F32)
    rows = sb("rows", [1, 512 * 4], F32)
    G = sb("Gg", [128, 128], F32)
    jnk = sb("jnk", [128, 128], F32)
    skv_in = sb("skv_in", [128, 16], F32)
    sk_in = sb("sk_in", [128, 16], F32)
    sk_tmp = sb("sk_tmp", [128, 16], F32)
    wm16 = sb("wm16", [16, 4], F32)
    wm4 = sb("wm4", [128, 16], F32)
    wring = [sb(f"wrs{i}", [128, 512], BF16) for i in range(NSLOT)]

    ring = {"issued": 0, "marks": [], "slots": wring}

    def ring_load(block_ap):
        slots = ring["slots"]
        nslot = len(slots)
        i = ring["issued"]
        if i >= nslot:
            need = i - nslot
            mk = None
            for (thr, sem, val) in ring["marks"]:
                if thr >= need:
                    mk = (sem, val)
                    break
            assert mk is not None, "weight ring under-released"
            wait("y", mk[0], mk[1])
        slot = slots[i % nslot]
        gsem = s_wl[(i // 16) % 4]
        ins("y", lambda e, o=slot, a=block_ap: e.dma_start(out=o[:], in_=a), gsem, 16)
        ring["issued"] += 1
        return slot, (gsem, gsem.n), i

    def ring_release_through(idx, sem):
        ring["marks"].append((idx, sem, sem.n))
        if len(ring["marks"]) > 80:
            ring["marks"] = [m for m in ring["marks"] if m[0] >= ring["issued"] - len(ring["slots"]) - 1]

    init_es = ExitStack()
    AR = sb("AR", [16, TOK], F32, init_es)
    P16f = sb("P16f", [128, 256], F32, init_es)
    onesf = sb("onesf", [128, 1], F32, init_es)
    for dst, src in [(E16f, E16), (E4f, E4), (garrT, GARR), (maskT, MASK4),
                     (decs, dec16), (AR, AR16), (id128f, ID128)]:
        ins("y", lambda e, o=dst, a=src: e.dma_start(out=o[:], in_=a), s_ld, 16)
    ins("y", lambda e: e.dma_start(out=P16f[:], in_=P16), s_ld, 16)
    ins("y", lambda e: e.dma_start(out=onesf[:], in_=ONES), s_ld, 16)
    wait("v", s_ld)
    ins("v", lambda e: e.tensor_copy(out=E16b[:], in_=E16f[:]))
    ins("v", lambda e: e.tensor_copy(out=P16b[:], in_=P16f[:]))
    ins("v", lambda e: e.tensor_copy(out=onesb[:], in_=onesf[:]))
    ins("v", lambda e: e.memset(bc_stat[:], 0.0))
    ins("v", lambda e: e.memset(rankstb[:], 0.0))
    ins("v", lambda e: e.memset(bcLN2[:], 0.0))
    ins("v", lambda e: e.memset(stKV[:], 0.0))
    ins("v", lambda e: e.memset(stK[:], 0.0), s_v)
    ins("v", lambda e: e.memset(scol[:], 0.0), s_v)
    wait("s", s_v)
    ins("s", lambda e: e.activation(out=scol[0:16, :], in_=decs[:], func=AF.Sigmoid), s_s)
    wait("v", s_s)
    ins("v", lambda e: e.tensor_single_scalar(out=scol[0:16, :], in_=scol[0:16, :], scalar=1e-38, op=OP.max), s_v)
    wait("s", s_v)
    ins("s", lambda e: e.activation(out=lnac[:], in_=scol[0:16, :], func=AF.Ln), s_s)
    wait("v", s_s)
    ins("v", lambda e: e.tensor_single_scalar(out=lnac[:], in_=lnac[:], scalar=-1e30, op=OP.max), s_v)
    wait("s", s_v)
    ins("s", lambda e: e.activation(out=c16[:], in_=AR[:], func=AF.Exp, scale=lnac[:]), s_s)
    wait("v", s_s)
    ins("v", lambda e: e.memset(c16b[:], 0.0))
    ins("v", lambda e: e.tensor_copy(out=c16b[0:16, :], in_=c16[:]), s_v)
    wait("t", s_v)
    for h in range(16):
        ins("t", lambda e, hh=h: e.matmul(
            out=ps[0][0:128, hh:hh + 1], lhsT=E16f[:, hh * 128:(hh + 1) * 128],
            rhs=scol[:, 0:1], start=True, stop=True))
    ins("t", lambda e: e.matmul(out=ps[1][:, 0:128], lhsT=E16f[:, 0:128],
                                rhs=E16f[:, 0:128], start=True, stop=True), s_t)
    release_bank(ps[1], s_t)
    wait("v", s_t)
    ins("v", lambda e: e.tensor_copy(out=jnk[:], in_=E16f[:, 0:128]))
    ins("v", lambda e: e.tensor_copy(out=decay_all[:], in_=ps[0][0:128, 0:16]), s_v)
    release_bank(ps[0], s_v)
    barrier()
    init_es.close()

    p0 = {"f32": [sb(f"p0f{i}", [128, 512], F32) for i in range(2)],
          "bf": [sb(f"p0b{i}", [128, 512], BF16) for i in range(2)],
          "i": 0, "tog": 0, "fin": {}, "guard": {0: 0, 1: 0},
          "cguard": {0: None, 1: None}}

    def p0_block(src_ap, dblk):
        i = p0["i"] % 2
        p0["i"] += 1
        f32b, bfb = p0["f32"][i], p0["bf"][i]
        if p0["cguard"][i] is not None:
            wait("s", p0["cguard"][i][0], p0["cguard"][i][1])
        ins("s", lambda e, o=f32b, a=src_ap: e.dma_start(out=o[:], in_=a), s_ldp[i], 16)
        ce = "v" if (p0["tog"] % 2 == 0) else "s"
        p0["tog"] += 1
        wait(ce, s_ldp[i])
        wait(ce, s_p0[i], p0["guard"][i])
        if ce == "v":
            ins("v", lambda e, o=bfb, a=f32b: e.tensor_copy(out=o[:], in_=a[:]), s_v)
            p0["cguard"][i] = (s_v, s_v.n)
            wait("g", s_v)
        else:
            ins("s", lambda e, o=bfb, a=f32b: e.copy(out=o[:], in_=a[:]), s_s)
            p0["cguard"][i] = (s_s, s_s.n)
            wait("g", s_s)
        ins("g", lambda e, d=dblk, b=bfb: e.dma_start(out=d, in_=b[:]), s_p0[i], 16)
        p0["guard"][i] = s_p0[i].n
        return (s_p0[i], s_p0[i].n)

    p0["blkmark"] = {}

    def p0_wlike(name, W, scratch, mg_lo, mg_hi, key=None):
        for mg in range(mg_lo, mg_hi):
            for k in range(NK):
                src2 = W[k * 128:(k + 1) * 128, mg * 512:(mg + 1) * 512]
                mk = p0_block(src2, scratch.ap()[mg * 16 + k])
                if key is not None:
                    p0["blkmark"][(key, mg * 16 + k)] = mk

    def p0_wlike_thunks(W, scratch, mg_lo, mg_hi):
        th = []
        for mg in range(mg_lo, mg_hi):
            for k in range(NK):
                src2 = W[k * 128:(k + 1) * 128, mg * 512:(mg + 1) * 512]
                th.append((src2, scratch.ap()[mg * 16 + k]))
        return th

    def p0_wd_thunks(s_lo, s_hi):
        th = []
        for s4 in range(s_lo, s_hi):
            for mg in range(4):
                for ci in range(16):
                    c = 16 * s4 + ci
                    j, m = c // 4, c % 4
                    src2 = w_down[512 * j + m: 512 * j + m + 509: 4, mg * 512:(mg + 1) * 512]
                    idx = s4 * 64 + mg * 16 + ci
                    th.append((src2, wd_bf.ap()[idx]))
        return th

    def p0_fin(name):
        p0["fin"][name] = (s_p0[0].n, s_p0[1].n)

    def ln_rows_and_bcast(store_isrow=None, store_misrow=None):
        tv = s_t.n
        wait("v", s_t, tv)
        inv_n = 1.0 / D
        mu = rows[0:1, 0:512]
        e2 = rows[0:1, 512:1024]
        var = rows[0:1, 1024:1536]
        mis = rows[0:1, 1536:2048]
        ins("v", lambda e: e.tensor_single_scalar(out=mu, in_=ps_st[0:1, 0:512], scalar=inv_n, op=OP.mult))
        ins("v", lambda e: e.tensor_single_scalar(out=e2, in_=ps_st[0:1, 512:1024], scalar=inv_n, op=OP.mult))
        ins("v", lambda e: e.tensor_tensor(out=var, in0=mu, in1=mu, op=OP.mult))
        ins("v", lambda e: e.tensor_tensor(out=var, in0=e2, in1=var, op=OP.subtract))
        ins("v", lambda e: e.tensor_single_scalar(out=var, in_=var, scalar=EPS_LN, op=OP.add), s_v)
        release_bank(ps_st, s_v)
        wait("s", s_v)
        ins("s", lambda e: e.activation(out=e2, in_=var, func=AF.Ln), s_s)
        isdst = bc_stat[0:1, 0:512] if store_isrow is None else store_isrow
        wait("s", s_s)
        wait("s", s_t)
        ins("s", lambda e, d=isdst: e.activation(out=d, in_=e2, func=AF.Exp, scale=-0.5), s_s)
        wait("v", s_s)
        wait("v", s_gd)
        ins("v", lambda e, d=isdst: e.tensor_tensor(out=mis, in0=mu, in1=d, op=OP.mult), s_v)
        wait("g", s_v)
        misdst = bc_stat[1:2, 0:512] if store_misrow is None else store_misrow
        ins("g", lambda e, d=misdst: e.dma_start(out=d, in_=mis), s_gd, 16)
        if store_isrow is None:
            wait("t", s_gd)
            bcast_pair_from(bc_stat[:, 0:512])

    def bcast_pair_from(bc_slice):
        use_bank(ps[0])
        use_bank(ps[1])
        ins("t", lambda e, a=bc_slice: e.matmul(out=ps[0][:], lhsT=E16f[:, 0:128], rhs=a, start=True, stop=True))
        ins("t", lambda e, a=bc_slice: e.matmul(out=ps[1][:], lhsT=E16f[:, 128:256], rhs=a, start=True, stop=True), s_t)
        wait("v", s_t)
        ins("v", lambda e: e.tensor_copy(out=is_b[:], in_=ps[0][:]))
        ins("v", lambda e: e.tensor_copy(out=mis_b[:], in_=ps[1][:]), s_v)
        release_bank(ps[0], s_v)
        release_bank(ps[1], s_v)

    p0_wlike("wq", w_qkv, wq_bf, 0, 12, key="wq")
    p0_fin("wq")

    a_es = ExitStack()
    xc = [sb(f"xc{i}", [128, TA], F32, a_es) for i in range(4)]
    xcb = [sb(f"xcb{i}", [128, TA], BF16, a_es) for i in range(2)]
    xcs = [sb(f"xcs{i}", [128, TA], BF16, a_es) for i in range(2)]
    bcLN1 = sb("bcLN1", [128, TOK], F32, a_es)
    ln1x2 = [sb(f"ln1x{i}", [128, NK * TA], BF16, a_es) for i in range(2)]
    qtile = sb("qtile", [128, H * TA], BF16, a_es)
    ktile = sb("ktile", [128, H * TA], BF16, a_es)
    Abuf = [sb(f"Abuf{i}", [128, TA], BF16, a_es) for i in range(2)]
    prodb = sb("prodb", [128, H * TA], BF16, a_es)
    kvt = sb("kvt", [128, TA + 1], F32, a_es)
    kvloc = sb("kvloc", [128, TA + 1], F32, a_es)
    ksloc = sb("ksloc", [128, TA + 1], F32, a_es)
    phs = sb("phs", [128, TA + 1], F32, a_es)
    ksin = phs
    tmpa = sb("tmpa", [128, TA], F32, a_es)

    marks = {}
    ins("v", lambda e: e.memset(bcLN1[:], 0.0), s_v)

    for ta in range(NTA):
        t0 = ta * TA
        for k in range(NK):
            xck = xc[k % 4]
            gk = ("ln1c", ta, k - 4) if k >= 4 else (("ln1c", ta - 1, 12 + k) if ta > 0 else None)
            if gk is not None and gk in marks:
                wait("y", s_v, marks[gk][0])
                wait("y", s_s, marks[gk][1])
            xin = x_fm[k * 128:(k + 1) * 128, t0:t0 + TA]
            ins("y", lambda e, a=xin, o=xck: e.dma_start(out=o[:], in_=a), s_l2[k % 4], 16)
            cb, sq = xcb[k % 2], xcs[k % 2]
            gk2 = ("ln1col", ta, k - 2) if k >= 2 else (("ln1col", ta - 1, 14 + k) if ta > 0 else None)
            wait("v", s_l2[k % 4])
            wait("s", s_l2[k % 4])
            if gk2 is not None and gk2 in marks:
                wait("v", s_t, marks[gk2])
                wait("s", s_t, marks[gk2])
            ins("v", lambda e, o=cb, a=xck: e.tensor_copy(out=o[:], in_=a[:]), s_v)
            ins("s", lambda e, o=sq, a=xck: e.activation(out=o[:], in_=a[:], func=AF.Square), s_s)
            vv, sv = s_v.n, s_s.n
            marks[("ln1c", ta, k)] = (s_v.n, s_s.n)
            if k == 0:
                use_bank(ps_st)
            wait("t", s_v, vv)
            wait("t", s_s, sv)
            ins("t", lambda e, a=cb, st=(k == 0), sp=(k == NK - 1): e.matmul(
                out=ps_st[0:1, 0:512], lhsT=onesb[:], rhs=a[:], start=st, stop=sp))
            ins("t", lambda e, a=sq, st=(k == 0), sp=(k == NK - 1): e.matmul(
                out=ps_st[0:1, 512:1024], lhsT=onesb[:], rhs=a[:], start=st, stop=sp), s_t)
            marks[("ln1col", ta, k)] = s_t.n
        ln_rows_and_bcast(store_isrow=bcLN1[0:1, t0:t0 + TA], store_misrow=bcLN1[1:2, t0:t0 + TA])

    def a_apply(ta):
        t0 = ta * TA
        steps = []
        def bcast_step():
            wait("t", s_gd)
            bcast_pair_from(bcLN1[:, t0:t0 + TA])
            marks[("a_bc", ta)] = s_v.n
        steps.append(bcast_step)
        for k in range(NK):
            def chunk(k=k):
                xck = xc[k % 4]
                if k >= 4:
                    gk = ("ln1ap", ta, k - 4)
                elif ta > 0:
                    gk = ("ln1ap", ta - 1, 12 + k)
                else:
                    gk = ("ln1c", NTA - 1, 12 + k)
                if gk in marks:
                    mv = marks[gk]
                    if isinstance(mv, tuple):
                        wait("g", s_v, mv[0])
                        wait("g", s_s, mv[1])
                    else:
                        wait("g", s_v, mv)
                xin = x_fm[k * 128:(k + 1) * 128, t0:t0 + TA]
                ins("g", lambda e, a=xin, o=xck: e.dma_start(out=o[:], in_=a), s_ax[k % 4], 16)
                wait("v", s_ax[k % 4])
                if k == 0:
                    wait("v", s_v, marks[("a_bc", ta)])
                ins("v", lambda e, a=xck: e.tensor_tensor(out=tmpa[:], in0=a[:], in1=is_b[:], op=OP.mult))
                ins("v", lambda e, k=k, o=ln1x2[ta % 2]: e.tensor_tensor(
                    out=o[:, k * TA:(k + 1) * TA], in0=tmpa[:], in1=mis_b[:], op=OP.subtract), s_v)
                marks[("ln1ap", ta, k)] = s_v.n
            steps.append(chunk)
        def fin():
            marks[("ln1x_rdy", ta)] = s_v.n
        steps.append(fin)
        return steps

    for st_ in a_apply(0):
        st_()
    marks[("ln1x_rdy", 0)] = s_v.n

    prep_q = (p0_wlike_thunks(w_out, wo_bf, 0, 4) + p0_wlike_thunks(w_gate, wg_bf, 0, 16)
              + p0_wlike_thunks(w_up, wu_bf, 0, 16) + p0_wd_thunks(0, 4))
    prep_i = [0]
    for ta in range(NTA):
        t0 = ta * TA
        ln1x = ln1x2[ta % 2]
        nxt = a_apply(ta + 1) if ta + 1 < NTA else []
        nxt_i = 0
        if ta > 0:
            wait("v", s_qsp, marks[("qsp", ta - 1)])
        wait("t", s_v, marks[("ln1x_rdy", ta)])
        for mq in range(48):
            mg = mq // 4
            if mq % 4 == 0:
                grp_blocks = []
                for k in range(NK):
                    if ta == 0:
                        bm = p0["blkmark"][("wq", mg * 16 + k)]
                        wait("y", bm[0], bm[1])
                    grp_blocks.append(ring_load(wq_bf.ap()[mg * 16 + k]))
                wait("t", grp_blocks[-1][1][0], grp_blocks[-1][1][1])
                for _ in range(18):
                    if prep_i[0] < len(prep_q):
                        p0_block(*prep_q[prep_i[0]])
                        prep_i[0] += 1
            if mq % 3 == 0 and nxt_i < len(nxt):
                nxt[nxt_i]()
                nxt_i += 1
            pb = ps[mq % 5]
            use_bank(pb)
            for k in range(NK):
                ins("t", lambda e, w=grp_blocks[k][0], k=k, mq=mq, p=pb, lx=ln1x: e.matmul(
                    out=p[:], lhsT=w[:, (mq % 4) * 128:(mq % 4 + 1) * 128],
                    rhs=lx[:, k * TA:(k + 1) * TA],
                    start=(k == 0), stop=(k == NK - 1)),
                    s_t if k == NK - 1 else None)
            if mq % 4 == 3:
                ring_release_through(grp_blocks[-1][2], s_t)
            mm_done = s_t.n
            h = mq % 16
            if mq < 16:
                wait("v", s_t, mm_done)
                ins("v", lambda e, p=pb: e.tensor_single_scalar(out=phs[:, 0:TA], in_=p[:], scalar=0.0, op=OP.min), s_v)
                wait("s", s_v)
                ins("s", lambda e: e.activation(out=kvloc[:, 0:TA], in_=phs[:, 0:TA], func=AF.Exp), s_s)
                ins("v", lambda e, p=pb: e.tensor_single_scalar(out=tmpa[:], in_=p[:], scalar=0.0, op=OP.max), s_v)
                release_bank(pb, s_v)
                wait("v", s_s)
                ins("v", lambda e, h=h: e.tensor_tensor(out=qtile[:, h * TA:(h + 1) * TA], in0=kvloc[:, 0:TA], in1=tmpa[:], op=OP.add), s_v)
            elif mq < 32:
                wait("v", s_t, mm_done)
                ins("v", lambda e, p=pb: e.tensor_single_scalar(out=phs[:, 0:TA], in_=p[:], scalar=0.0, op=OP.min), s_v)
                wait("s", s_v)
                ins("s", lambda e: e.activation(out=kvloc[:, 0:TA], in_=phs[:, 0:TA], func=AF.Exp), s_s)
                ins("v", lambda e, p=pb: e.tensor_single_scalar(out=tmpa[:], in_=p[:], scalar=0.0, op=OP.max), s_v)
                release_bank(pb, s_v)
                wait("v", s_s)
                ins("v", lambda e, h=h: e.tensor_tensor(out=ktile[:, h * TA:(h + 1) * TA], in0=kvloc[:, 0:TA], in1=tmpa[:], op=OP.add))
                ins("v", lambda e, h=h: e.tensor_copy(out=ksin[:, 0:1], in_=stK[:, h:h + 1]))
                ins("v", lambda e, h=h: e.tensor_copy(out=ksin[:, 1:TA + 1], in_=ktile[:, h * TA:(h + 1) * TA]))
                ins("v", lambda e, h=h: e.tensor_tensor_scan(
                    out=ksloc[:], data0=decay_all[:, h:h + 1].broadcast_to([128, TA + 1]),
                    data1=ksin[:], initial=0.0, op0=OP.mult, op1=OP.add))
                ins("v", lambda e, h=h: e.tensor_tensor(out=prodb[:, h * TA:(h + 1) * TA], in0=qtile[:, h * TA:(h + 1) * TA], in1=ksloc[:, 1:TA + 1], op=OP.mult))
                ins("v", lambda e, h=h: e.tensor_copy(out=stK[:, h:h + 1], in_=ksloc[:, TA:TA + 1]), s_v)
            else:
                wait("v", s_t, mm_done)
                ins("v", lambda e, h=h: e.tensor_copy(out=kvt[:, 0:1], in_=stKV[:, h:h + 1]))
                ins("v", lambda e, p=pb, h=h: e.tensor_tensor(out=kvt[:, 1:TA + 1], in0=ktile[:, h * TA:(h + 1) * TA], in1=p[:], op=OP.mult), s_v)
                release_bank(pb, s_v)
                ins("v", lambda e, h=h: e.tensor_tensor_scan(
                    out=kvloc[:], data0=decay_all[:, h:h + 1].broadcast_to([128, TA + 1]),
                    data1=kvt[:], initial=0.0, op0=OP.mult, op1=OP.add))
                ab = Abuf[h % 2]
                gk = ("Asp", ta, h - 2) if h >= 2 else (("Asp", ta - 1, 14 + h) if ta > 0 else None)
                if gk is not None and gk in marks:
                    wait("v", s_sp[h % 2], marks[gk])
                ins("v", lambda e, h=h, ab=ab: e.tensor_tensor(out=ab[:], in0=qtile[:, h * TA:(h + 1) * TA], in1=kvloc[:, 1:TA + 1], op=OP.mult))
                ins("v", lambda e, h=h: e.tensor_copy(out=stKV[:, h:h + 1], in_=kvloc[:, TA:TA + 1]), s_v)
                wait("g", s_v)
                ins("g", lambda e, h=h, ab=ab, ta=ta: e.dma_start(out=A_sp.ap()[ta][:, h * TA:(h + 1) * TA], in_=ab[:]), s_sp[h % 2], 16)
                marks[("Asp", ta, h)] = s_sp[h % 2].n
        while nxt_i < len(nxt):
            nxt[nxt_i]()
            nxt_i += 1
        if ta + 1 < NTA:
            marks[("ln1x_rdy", ta + 1)] = s_v.n
        use_bank(ps_den)
        wait("t", s_v)
        for h in range(16):
            ins("t", lambda e, h=h: e.matmul(
                out=ps_den[:], lhsT=P16b[:, h * 16:(h + 1) * 16], rhs=prodb[:, h * TA:(h + 1) * TA],
                start=(h == 0), stop=(h == 15)), s_t if h == 15 else None)
        marks[("denmm", ta)] = s_t.n
        wait("v", s_t)
        ins("v", lambda e, t0=t0: e.tensor_copy(out=den_loc[:, t0:t0 + TA], in_=ps_den[:]), s_v)
        release_bank(ps_den, s_v)
        wait("g", s_v)
        ins("g", lambda e, ta=ta: e.dma_start(out=q_sp.ap()[ta], in_=qtile[:]), s_qsp, 16)
        marks[("qsp", ta)] = s_qsp.n
        if ta == 0:
            p0["fin"]["wo"] = (s_p0[0].n, s_p0[1].n) if prep_i[0] >= 64 else None

    while prep_i[0] < len(prep_q):
        p0_block(*prep_q[prep_i[0]])
        prep_i[0] += 1
    p0["fin"]["wo"] = (s_p0[0].n, s_p0[1].n)
    p0["fin"]["ffn"] = (s_p0[0].n, s_p0[1].n)

    barrier()
    ins("g", lambda e: e.dma_start(out=cc_in.ap()[:, 0:16], in_=stKV[:]), s_gd, 16)
    ins("g", lambda e: e.dma_start(out=cc_in.ap()[:, 16:32], in_=stK[:]), s_gd, 16)
    wait("g", s_gd)
    ins("g", lambda e: e.collective_compute(
        "AllGather", OP.bypass, replica_groups=[[0, 1, 2, 3], [4, 5, 6, 7]],
        ins=[cc_in.ap().opt()], outs=[cc_out.ap().opt()]), s_cc, 1)
    wait("g", s_cc)
    for r in range(4):
        ins("g", lambda e, r=r: e.dma_start(out=G[:, r * 32:(r + 1) * 32], in_=cc_out.ap()[r]), s_gd, 16)
    gv = s_gd.n
    wait("s", s_gd, gv)
    ins("s", lambda e: e.activation(out=wm16[:], in_=garrT[:], func=AF.Exp, scale=lnac[:]), s_s)
    wait("v", s_s)
    ins("v", lambda e: e.tensor_tensor(out=wm16[:], in0=wm16[:], in1=maskT[:], op=OP.mult), s_v)
    ins("v", lambda e: e.memset(wm4[:], 0.0), s_v)
    wait("g", s_v)
    for r in range(4):
        ins("g", lambda e, r=r: e.dma_start(out=wm4[r:r + 1, 0:16], in_=wm16[0:16, r:r + 1]), s_wm, 16)
    wait("t", s_wm)
    use_bank(ps[2])
    use_bank(ps[3])
    for r in range(4):
        ins("t", lambda e, r=r: e.matmul(out=ps[2][0:128, r * 16:(r + 1) * 16],
                                         lhsT=E16f[:, r * 128:(r + 1) * 128],
                                         rhs=wm4[:], start=True, stop=True))
    ins("t", lambda e: e.matmul(out=ps[3][:, 0:128], lhsT=E16f[:, 0:128],
                                rhs=E16f[:, 0:128], start=True, stop=True), s_t)
    release_bank(ps[3], s_t)
    wait("v", s_t)
    ins("v", lambda e: e.tensor_copy(out=jnk[:], in_=E16f[:, 0:128]))
    for r in range(4):
        if r == 0:
            ins("v", lambda e, r=r: e.tensor_tensor(out=skv_in[:], in0=G[:, r * 32:r * 32 + 16], in1=ps[2][0:128, r * 16:r * 16 + 16], op=OP.mult))
            ins("v", lambda e, r=r: e.tensor_tensor(out=sk_in[:], in0=G[:, r * 32 + 16:r * 32 + 32], in1=ps[2][0:128, r * 16:r * 16 + 16], op=OP.mult))
        else:
            ins("v", lambda e, r=r: e.tensor_tensor(out=sk_tmp[:], in0=G[:, r * 32:r * 32 + 16], in1=ps[2][0:128, r * 16:r * 16 + 16], op=OP.mult))
            ins("v", lambda e: e.tensor_tensor(out=skv_in[:], in0=skv_in[:], in1=sk_tmp[:], op=OP.add))
            ins("v", lambda e, r=r: e.tensor_tensor(out=sk_tmp[:], in0=G[:, r * 32 + 16:r * 32 + 32], in1=ps[2][0:128, r * 16:r * 16 + 16], op=OP.mult))
            ins("v", lambda e: e.tensor_tensor(out=sk_in[:], in0=sk_in[:], in1=sk_tmp[:], op=OP.add))
    ins("v", lambda e: e.engine_nop(), s_v)
    release_bank(ps[2], s_v)
    ins("v", lambda e: e.memset(kspad[:], 0.0))
    ins("v", lambda e: e.tensor_copy(out=kspad[:, 0:16], in_=sk_in[:]))
    for h in range(15, 0, -1):
        ins("v", lambda e, h=h: e.tensor_copy(out=kspad[:, 16 * h + h:16 * h + h + 1], in_=kspad[:, h:h + 1]))
    ins("v", lambda e: e.memset(kspad[:, 1:16], 0.0), s_v)
    wait("g", s_v)
    for h in range(16):
        ins("g", lambda e, h=h: e.dma_start(out=G[h:h + 1, 0:128], in_=skv_in[:, h:h + 1],
                                            allow_slow_non_contiguous=True), s_wm, 16)
    wait("v", s_wm)
    for h in range(16):
        ins("v", lambda e, h=h: e.tensor_tensor(out=rankstb[0:16, h * 128:(h + 1) * 128],
                                                in0=G[0:16, 0:128], in1=E16f[0:16, h * 128:(h + 1) * 128], op=OP.mult),
            s_v if h == 15 else None)
    barrier()
    a_es.close()

    c1_es = ExitStack()
    qtc = sb("qtc", [128, H * TC], BF16, c1_es)
    Atc = sb("Atc", [128, H * TC], BF16, c1_es)
    attnt2 = [sb(f"attnt{i}", [128, H * TC], BF16, c1_es) for i in range(2)]
    yt = sb("yt", [128, NK * TC], F32, c1_es)
    ybc = [sb(f"ybc{i}", [128, TC], BF16, c1_es) for i in range(2)]
    ysq = [sb(f"ysq{i}", [128, TC], BF16, c1_es) for i in range(2)]
    x2bc = [sb(f"x2bc{i}", [128, TC], BF16, c1_es) for i in range(2)]
    x2sq = [sb(f"x2sq{i}", [128, TC], BF16, c1_es) for i in range(2)]
    dent = sb("dent", [16, TC], F32, c1_es)
    r16b = sb("r16b", [128, TC], BF16, c1_es)
    tmpc = sb("tmpc", [128, TC], F32, c1_es)

    wait("y", s_p0[0], p0["fin"]["wo"][0])
    wait("y", s_p0[1], p0["fin"]["wo"][1])
    ins("v", lambda e: e.memset(r16b[:], 0.0), s_v)

    def c1_head(tc):
        t0 = tc * TC
        attnt = attnt2[tc % 2]
        if tc > 0:
            wait("y", s_v, marks[("c1_free", tc - 1)])
        ins("y", lambda e, tc=tc: e.dma_start(out=qtc[:], in_=q_sp.ap()[tc]), s_qa, 16)
        ins("y", lambda e, tc=tc: e.dma_start(out=Atc[:], in_=A_sp.ap()[tc]), s_qa, 16)
        qa_ready = s_qa.n
        use_bank(ps_den)
        wait("t", s_qa, qa_ready)
        for h in range(16):
            ins("t", lambda e, h=h: e.matmul(out=ps_den[:], lhsT=kspad[:, h * 16:(h + 1) * 16],
                                             rhs=qtc[:, h * TC:(h + 1) * TC], start=(h == 0), stop=(h == 15)),
                s_t if h == 15 else None)
        wait("v", s_t)
        ins("v", lambda e, t0=t0: e.tensor_tensor(out=dent[:], in0=c16[:, t0:t0 + TC], in1=ps_den[:], op=OP.mult), s_v)
        release_bank(ps_den, s_v)
        ins("v", lambda e, t0=t0: e.tensor_tensor(out=dent[:], in0=dent[:], in1=den_loc[:, t0:t0 + TC], op=OP.add))
        ins("v", lambda e: e.tensor_single_scalar(out=dent[:], in_=dent[:], scalar=EPS_DEN, op=OP.max))
        ins("v", lambda e: e.reciprocal(out=dent[:], in_=dent[:]))
        ins("v", lambda e: e.tensor_copy(out=r16b[0:16, :], in_=dent[:]), s_v)
        rdy = s_v.n
        if tc > 1:
            wait("v", s_t, marks[("wout_mm", tc - 2)])
        wait("t", s_v, rdy)
        for h in range(16):
            pc = ps[2] if h % 2 == 0 else ps[3]
            pr = ps[4]
            use_bank(pc)
            ins("t", lambda e, h=h, t0=t0, p=pc: e.matmul(
                out=p[:], lhsT=rankstb[:, h * 128:(h + 1) * 128], rhs=c16b[:, t0:t0 + TC], start=True, stop=True), s_t)
            use_bank(pr)
            ins("t", lambda e, h=h, p=pr: e.matmul(
                out=p[:], lhsT=E16b[:, h * 128:(h + 1) * 128], rhs=r16b[:], start=True, stop=True), s_t)
            mmv = s_t.n
            wait("v", s_t, mmv)
            ins("v", lambda e, h=h, p=pc: e.tensor_tensor(out=tmpc[:], in0=qtc[:, h * TC:(h + 1) * TC], in1=p[:], op=OP.mult), s_v)
            release_bank(pc, s_v)
            ins("v", lambda e, h=h: e.tensor_tensor(out=tmpc[:], in0=tmpc[:], in1=Atc[:, h * TC:(h + 1) * TC], op=OP.add))
            ins("v", lambda e, h=h, a=attnt, p=pr: e.tensor_tensor(out=a[:, h * TC:(h + 1) * TC], in0=tmpc[:], in1=p[:], op=OP.mult), s_v)
            release_bank(pr, s_v)
        marks[("c1_free", tc)] = s_v.n
        marks[("attn_rdy", tc)] = s_v.n

    def c1_wout(tc):
        attnt = attnt2[tc % 2]
        wait("t", s_v, marks[("attn_rdy", tc)])
        for mo in range(16):
            mg = mo // 4
            if mo % 4 == 0:
                grp_blocks = [ring_load(wo_bf.ap()[mg * 16 + k]) for k in range(NK)]
                wait("t", grp_blocks[-1][1][0], grp_blocks[-1][1][1])
            pb = ps[mo % 2]
            use_bank(pb)
            for k in range(NK):
                ins("t", lambda e, w=grp_blocks[k][0], k=k, mo=mo, p=pb: e.matmul(
                    out=p[:], lhsT=w[:, (mo % 4) * 128:(mo % 4 + 1) * 128],
                    rhs=attnt[:, k * TC:(k + 1) * TC], start=(k == 0), stop=(k == NK - 1)),
                    s_t if k == NK - 1 else None)
            if mo % 4 == 3:
                ring_release_through(grp_blocks[-1][2], s_t)
            mmv = s_t.n
            yb, sq = ybc[mo % 2], ysq[mo % 2]
            gk = ("c1col", tc, mo - 2) if mo >= 2 else (("c1col", tc - 1, 14 + mo) if tc > 0 else None)
            if gk is not None and gk in marks:
                wait("s", s_t, marks[gk])
            wait("s", s_t, mmv)
            ins("s", lambda e, mo=mo, p=pb: e.copy(out=yt[:, mo * TC:(mo + 1) * TC], in_=p[:]))
            ins("s", lambda e, p=pb, o=yb: e.copy(out=o[:], in_=p[:]))
            ins("s", lambda e, p=pb, o=sq: e.activation(out=o[:], in_=p[:], func=AF.Square), s_s)
            release_bank(pb, s_s)
            sv = s_s.n
            if mo == 0:
                use_bank(ps_st)
            wait("t", s_s, sv)
            ins("t", lambda e, a=yb, st=(mo == 0), sp=(mo == 15): e.matmul(
                out=ps_st[0:1, 0:512], lhsT=onesb[:], rhs=a[:], start=st, stop=sp))
            ins("t", lambda e, a=sq, st=(mo == 0), sp=(mo == 15): e.matmul(
                out=ps_st[0:1, 512:1024], lhsT=onesb[:], rhs=a[:], start=st, stop=sp), s_t)
            marks[("c1col", tc, mo)] = s_t.n
        marks[("wout_mm", tc)] = s_t.n

    def c1_tail(tc):
        t0 = tc * TC
        ln_rows_and_bcast()
        wait("v", s_v)
        for k in range(NK):
            ins("v", lambda e, k=k: e.tensor_tensor(out=tmpc[:], in0=yt[:, k * TC:(k + 1) * TC], in1=is_b[:], op=OP.mult))
            ins("v", lambda e, k=k: e.tensor_tensor(out=yt[:, k * TC:(k + 1) * TC], in0=tmpc[:], in1=mis_b[:], op=OP.subtract),
                s_v if k == NK - 1 else None)
        wait("g", s_v)
        for k in range(NK):
            xin = x_fm[k * 128:(k + 1) * 128, t0:t0 + TC]
            ins("g", lambda e, k=k, a=xin: e.dma_start(out=yt[:, k * TC:(k + 1) * TC], in_=a, accum_op=OP.add), s_ac, 16)
        acc_done = s_ac.n
        wait("s", s_ac, acc_done)
        wait("v", s_ac, acc_done)
        for k in range(NK):
            xb2, sq2 = x2bc[k % 2], x2sq[k % 2]
            gk = ("ln2col", tc, k - 2) if k >= 2 else (("ln2col", tc - 1, 14 + k) if tc > 0 else None)
            if gk is not None and gk in marks:
                wait("s", s_t, marks[gk])
                wait("v", s_t, marks[gk])
            gk2 = ("x2spill", tc, k - 2) if k >= 2 else (("x2spill", tc - 1, 14 + k) if tc > 0 else None)
            if gk2 is not None and gk2 in marks:
                wait("s", s_x2s[k % 2], marks[gk2])
            ins("s", lambda e, k=k, o=xb2: e.copy(out=o[:], in_=yt[:, k * TC:(k + 1) * TC]), s_s)
            wait("v", s_s)
            ins("v", lambda e, o=sq2, a=xb2: e.tensor_tensor(out=o[:], in0=a[:], in1=a[:], op=OP.mult), s_v)
            sv, vv = s_s.n, s_v.n
            wait("g", s_s, sv)
            ins("g", lambda e, k=k, a=xb2, tc=tc: e.dma_start(out=x2_sp.ap()[k][:, tc * TC:(tc + 1) * TC], in_=a[:]), s_x2s[k % 2], 16)
            marks[("x2spill", tc, k)] = s_x2s[k % 2].n
            if k == 0:
                use_bank(ps_st)
            wait("t", s_s, sv)
            wait("t", s_v, vv)
            ins("t", lambda e, a=xb2, st=(k == 0), sp=(k == NK - 1): e.matmul(
                out=ps_st[0:1, 0:512], lhsT=onesb[:], rhs=a[:], start=st, stop=sp))
            ins("t", lambda e, a=sq2, st=(k == 0), sp=(k == NK - 1): e.matmul(
                out=ps_st[0:1, 512:1024], lhsT=onesb[:], rhs=a[:], start=st, stop=sp), s_t)
            marks[("ln2col", tc, k)] = s_t.n
        ln_rows_and_bcast(store_isrow=bcLN2[0:1, t0:t0 + TC], store_misrow=bcLN2[1:2, t0:t0 + TC])

    c1_head(0)
    for tc in range(NTC):
        c1_wout(tc)
        if tc + 1 < NTC:
            c1_head(tc + 1)
        c1_tail(tc)

    barrier()
    c1_es.close()

    c2_es = ExitStack()
    wring2 = [sb(f"wr2_{i}", [128, 512], BF16, c2_es) for i in range(14)]
    ring["slots"] = wring + wring2
    ring["issued"] = 0
    ring["marks"] = [(10 ** 9, s_t, s_t.n)]
    ring["marks"] = []
    ln2x = sb("ln2x", [128, NK * T2], BF16, c2_es)
    xv2 = [sb(f"xv2_{i}", [128, T2], BF16, c2_es) for i in range(4)]
    hslab = sb("hslab", [128, 16 * T2], BF16, c2_es)
    facc = sb("facc", [128, NK * T2], F32, c2_es)
    hm = [sb(f"hm{i}", [128, T2], F32, c2_es) for i in range(4)]
    gs = [sb(f"gs{i}", [128, T2], F32, c2_es) for i in range(4)]
    m2t = sb("m2t", [128, T2], F32, c2_es)
    m2u = sb("m2u", [128, T2], F32, c2_es)
    yb3 = [sb(f"yb3_{i}", [128, T2], BF16, c2_es) for i in range(2)]
    ys3 = [sb(f"ys3_{i}", [128, T2], BF16, c2_es) for i in range(2)]
    obuf = [sb(f"obuf{i}", [128, T2], F32, c2_es) for i in range(2)]
    tmp2 = sb("tmp2", [128, T2], F32, c2_es)

    wait("y", s_p0[0], p0["fin"]["ffn"][0])
    wait("y", s_p0[1], p0["fin"]["ffn"][1])
    for t2 in range(NT2):
        t0 = t2 * T2
        bcast_pair_from(bcLN2[:, t0:t0 + T2])
        bcv = s_v.n
        if t2 > 0:
            wait("v", s_t, marks[("ffn_mm", t2 - 1)])
            wait("v", s_s, marks[("ln3cast", t2 - 1)])
        for k in range(NK):
            xb2 = xv2[k % 4]
            gk = ("ln2x", t2, k - 4) if k >= 4 else (("fout_v", t2 - 1, 12 + k) if t2 > 0 else None)
            if gk is not None and gk in marks:
                wait("y", s_v, marks[gk])
            ins("y", lambda e, k=k, t0=t0, o=xb2: e.dma_start(out=o[:], in_=x2_sp.ap()[k][:, t0:t0 + T2]), s_l2[k % 4], 16)
            wait("v", s_l2[k % 4])
            if k == 0:
                wait("v", s_v, bcv)
            ins("v", lambda e, k=k, a=xb2: e.tensor_tensor(out=tmp2[:], in0=a[:], in1=is_b[:], op=OP.mult))
            ins("v", lambda e, k=k: e.tensor_tensor(out=ln2x[:, k * T2:(k + 1) * T2], in0=tmp2[:], in1=mis_b[:], op=OP.subtract), s_v)
            marks[("ln2x", t2, k)] = s_v.n
        ln2_ready = s_v.n
        for s4 in range(4):
            if s4 > 0:
                wait("v", s_t, marks[("down_mm", t2, s4 - 1)])
            for jj in range(4):
                j = 4 * s4 + jj
                if s4 + jj > 0:
                    wait("s", s_v, marks.get(("spars", t2, s4, jj - 1),
                                             marks.get(("spars", t2, s4 - 1, 3), 0)))
                gblocks = [ring_load(wg_bf.ap()[j * 16 + k]) for k in range(NK)]
                wait("t", gblocks[-1][1][0], gblocks[-1][1][1])
                for m in range(4):
                    pg = ps[[2, 3, 4, 0][m]]
                    use_bank(pg)
                    for k in range(NK):
                        if s4 == 0 and jj == 0 and m == 0:
                            wait("t", s_v, marks[("ln2x", t2, k)])
                        ins("t", lambda e, w=gblocks[k][0], k=k, m=m, p=pg: e.matmul(
                            out=p[:], lhsT=w[:, m::4], rhs=ln2x[:, k * T2:(k + 1) * T2],
                            start=(k == 0), stop=(k == NK - 1)), s_t if k == NK - 1 else None)
                    gv2 = s_t.n
                    wait("s", s_t, gv2)
                    ins("s", lambda e, m=m, p=pg: e.activation(out=gs[m][:], in_=p[:], func=AF.Silu), s_s)
                    release_bank(pg, s_s)
                ring_release_through(gblocks[-1][2], s_t)
                ublocks = [ring_load(wu_bf.ap()[j * 16 + k]) for k in range(NK)]
                wait("t", ublocks[-1][1][0], ublocks[-1][1][1])
                for m in range(4):
                    pu = ps[[2, 3, 4, 0][m]]
                    use_bank(pu)
                    for k in range(NK):
                        ins("t", lambda e, w=ublocks[k][0], k=k, m=m, p=pu: e.matmul(
                            out=p[:], lhsT=w[:, m::4], rhs=ln2x[:, k * T2:(k + 1) * T2],
                            start=(k == 0), stop=(k == NK - 1)), s_t if k == NK - 1 else None)
                    uv = s_t.n
                    wait("v", s_t, uv)
                    wait("v", s_s)
                    ins("v", lambda e, m=m, p=pu: e.tensor_tensor(out=hm[m][:], in0=gs[m][:], in1=p[:], op=OP.mult), s_v)
                    release_bank(pu, s_v)
                ring_release_through(ublocks[-1][2], s_t)
                hm_done = s_v.n
                wait("s", s_v, hm_done)
                for m in range(4):
                    ins("s", lambda e, m=m: e.activation(out=gs[m][:], in_=hm[m][:], func=AF.Square),
                        s_s if m == 3 else None)
                wait("v", s_s)
                ins("v", lambda e: e.tensor_tensor(out=m2t[:], in0=gs[0][:], in1=gs[1][:], op=OP.max))
                ins("v", lambda e: e.tensor_tensor(out=m2u[:], in0=gs[2][:], in1=gs[3][:], op=OP.max))
                ins("v", lambda e: e.tensor_tensor(out=m2t[:], in0=m2t[:], in1=m2u[:], op=OP.min))
                ins("v", lambda e: e.tensor_tensor(out=m2u[:], in0=gs[0][:], in1=gs[1][:], op=OP.min))
                ins("v", lambda e: e.tensor_tensor(out=tmp2[:], in0=gs[2][:], in1=gs[3][:], op=OP.min))
                ins("v", lambda e: e.tensor_tensor(out=m2u[:], in0=m2u[:], in1=tmp2[:], op=OP.max))
                ins("v", lambda e: e.tensor_tensor(out=m2t[:], in0=m2t[:], in1=m2u[:], op=OP.max))
                for m in range(4):
                    ins("v", lambda e, m=m: e.tensor_tensor(out=tmp2[:], in0=gs[m][:], in1=m2t[:], op=OP.is_ge))
                    ins("v", lambda e, m=m, jj=jj: e.tensor_tensor(
                        out=hslab[:, (4 * jj + m) * T2:(4 * jj + m + 1) * T2], in0=tmp2[:], in1=hm[m][:], op=OP.mult),
                        s_v if m == 3 else None)
                marks[("spars", t2, s4, jj)] = s_v.n
            hs_ready = s_v.n
            wait("t", s_v, hs_ready)
            for mg in range(4):
                dblocks = [ring_load(wd_bf.ap()[s4 * 64 + mg * 16 + ci]) for ci in range(16)]
                wait("t", dblocks[-1][1][0], dblocks[-1][1][1])
                for mo in range(4 * mg, 4 * mg + 4):
                    pd = ps[[2, 3, 4, 0][mo % 4]]
                    use_bank(pd)
                    for ci in range(16):
                        ins("t", lambda e, w=dblocks[ci][0], ci=ci, mo=mo, mg=mg, p=pd: e.matmul(
                            out=p[:], lhsT=w[:, (mo - 4 * mg) * 128:(mo - 4 * mg + 1) * 128],
                            rhs=hslab[:, ci * T2:(ci + 1) * T2], start=(ci == 0), stop=(ci == 15)),
                            s_t if ci == 15 else None)
                    dv = s_t.n
                    wait("v", s_t, dv)
                    if s4 == 0:
                        ins("v", lambda e, mo=mo, p=pd: e.tensor_copy(out=facc[:, mo * T2:(mo + 1) * T2], in_=p[:]), s_v)
                    else:
                        ins("v", lambda e, mo=mo, p=pd: e.tensor_tensor(
                            out=facc[:, mo * T2:(mo + 1) * T2], in0=facc[:, mo * T2:(mo + 1) * T2], in1=p[:], op=OP.add), s_v)
                    release_bank(pd, s_v)
                ring_release_through(dblocks[-1][2], s_t)
            marks[("down_mm", t2, s4)] = s_t.n
        ffn_ready = s_v.n
        marks[("ffn_mm", t2)] = s_t.n
        wait("s", s_v, ffn_ready)
        for k in range(NK):
            yb, sqq = yb3[k % 2], ys3[k % 2]
            gk = ("ln3col", t2, k - 2) if k >= 2 else (("ln3col", t2 - 1, 14 + k) if t2 > 0 else None)
            if gk is not None and gk in marks:
                wait("s", s_t, marks[gk])
            ins("s", lambda e, k=k, o=yb: e.copy(out=o[:], in_=facc[:, k * T2:(k + 1) * T2]))
            ins("s", lambda e, k=k, o=sqq: e.activation(out=o[:], in_=facc[:, k * T2:(k + 1) * T2], func=AF.Square), s_s)
            sv = s_s.n
            if k == 0:
                use_bank(ps_st)
            wait("t", s_s, sv)
            ins("t", lambda e, a=yb, st=(k == 0), sp=(k == NK - 1): e.matmul(
                out=ps_st[0:1, 0:512], lhsT=onesb[:], rhs=a[:], start=st, stop=sp))
            ins("t", lambda e, a=sqq, st=(k == 0), sp=(k == NK - 1): e.matmul(
                out=ps_st[0:1, 512:1024], lhsT=onesb[:], rhs=a[:], start=st, stop=sp), s_t)
            marks[("ln3col", t2, k)] = s_t.n
        marks[("ln3cast", t2)] = s_s.n
        ln_rows_and_bcast()
        wait("v", s_v)
        for k in range(NK):
            xb2 = xv2[k % 4]
            ob = obuf[k % 2]
            gko = ("fout_g", t2, k - 2) if k >= 2 else (("fout_g", t2 - 1, 14 + k) if t2 > 0 else None)
            if gko is not None and gko in marks:
                wait("v", s_go[k % 2], marks[gko])
            gkx = ("fout_v", t2, k - 4) if k >= 4 else (("ln2x", t2, 12 + k))
            if gkx in marks:
                wait("y", s_v, marks[gkx])
            ins("y", lambda e, k=k, t0=t0, o=xb2: e.dma_start(out=o[:], in_=x2_sp.ap()[k][:, t0:t0 + T2]), s_fo[k % 4], 16)
            ins("v", lambda e, k=k: e.tensor_tensor(out=tmp2[:], in0=facc[:, k * T2:(k + 1) * T2], in1=is_b[:], op=OP.mult))
            ins("v", lambda e, k=k: e.tensor_tensor(out=tmp2[:], in0=tmp2[:], in1=mis_b[:], op=OP.subtract))
            wait("v", s_fo[k % 4])
            ins("v", lambda e, k=k, a=xb2, o=ob: e.tensor_tensor(out=o[:], in0=tmp2[:], in1=a[:], op=OP.add), s_v)
            marks[("fout_v", t2, k)] = s_v.n
            wait("g", s_v)
            ins("g", lambda e, k=k, t0=t0, o=ob: e.dma_start(
                out=out_fm[k * 128:(k + 1) * 128, t0:t0 + T2], in_=o[:]), s_go[k % 2], 16)
            marks[("fout_g", t2, k)] = s_go[k % 2].n

    wait("g", s_go[0])
    wait("g", s_go[1])
    wait("g", s_gd)
    c2_es.close()

    with nc.Block() as block:
        @block.sync
        def _(e):
            for f in prog["y"]:
                f(e)
        @block.tensor
        def _(e):
            for f in prog["t"]:
                f(e)
        @block.vector
        def _(e):
            for f in prog["v"]:
                f(e)
        @block.scalar
        def _(e):
            for f in prog["s"]:
                f(e)
        @block.gpsimd
        def _(e):
            for f in prog["g"]:
                f(e)
    es.close()
    import sys as _s
    print(f"[kernel] auto-fences: v={fence_count['v']} s={fence_count['s']}", file=_s.stderr)
    return nc


_NC_CACHE = {}


def kernel(L0_attn_x, L0_attn_w_qkv, L0_attn_w_out, L0_attn_decay_param,
           L0_ffn_w_gate, L0_ffn_w_up, L0_ffn_w_down):
    x = np.asarray(L0_attn_x, np.float32)
    w_qkv_a = np.ascontiguousarray(np.asarray(L0_attn_w_qkv, np.float32))
    w_out_a = np.ascontiguousarray(np.asarray(L0_attn_w_out, np.float32))
    w_gate_a = np.ascontiguousarray(np.asarray(L0_ffn_w_gate, np.float32))
    w_up_a = np.ascontiguousarray(np.asarray(L0_ffn_w_up, np.float32))
    w_down_a = np.ascontiguousarray(np.asarray(L0_ffn_w_down, np.float32))
    dec = np.ascontiguousarray(np.asarray(L0_attn_decay_param, np.float32).reshape(16, 1))

    E16a = np.zeros((128, 16 * 128), np.float32)
    for h in range(16):
        E16a[h, h * 128:(h + 1) * 128] = 1.0
    E4a = np.zeros((4, 4 * 128), np.float32)
    for r in range(4):
        E4a[r, r * 128:(r + 1) * 128] = 1.0
    P16a = np.zeros((128, 16 * 16), np.float32)
    for h in range(16):
        P16a[:, h * 16 + h] = 1.0
    AR16a = np.tile(np.arange(1, TOK + 1, dtype=np.float32), (16, 1))
    ONESa = np.ones((128, 1), np.float32)
    ID128a = np.eye(128, dtype=np.float32)

    if "nc" not in _NC_CACHE:
        _NC_CACHE["nc"] = build_program()
    nc = _NC_CACHE["nc"]

    in_maps = []
    for c in range(8):
        seq, pos = c // 4, c % 4
        x_fm_c = np.ascontiguousarray(x[seq, pos * TOK:(pos + 1) * TOK, :].T)
        garr = np.zeros((16, 4), np.float32)
        mask = np.zeros((16, 4), np.float32)
        for r in range(4):
            if r < pos:
                garr[:, r] = 2048.0 * (pos - 1 - r)
                mask[:, r] = 1.0
        in_maps.append({
            "x_fm": x_fm_c, "w_qkv": w_qkv_a, "w_out": w_out_a, "w_gate": w_gate_a,
            "w_up": w_up_a, "w_down": w_down_a, "dec16": dec, "E16": E16a, "E4": E4a,
            "P16": P16a, "AR16": AR16a, "GARR": garr, "MASK4": mask,
            "ONES": ONESa, "ID128": ID128a,
        })
    res = run_bass_kernel_spmd(nc, in_maps, core_ids=list(range(8)))
    out = np.empty((B, T, D), np.float32)
    for c in range(8):
        seq, pos = c // 4, c % 4
        out[seq, pos * TOK:(pos + 1) * TOK, :] = res.results[c]["out_fm"].T
    return out
